# version 19
# baseline (speedup 1.0000x reference)
import contextlib
import math

import numpy as np

import concourse.bass as bass
import concourse.mybir as mybir
from concourse.bass_utils import run_bass_kernel_spmd

F32 = mybir.dt.float32
BF16 = mybir.dt.bfloat16
I32 = mybir.dt.int32
AX = mybir.AxisListType
ALU = mybir.AluOpType
ACTF = mybir.ActivationFunctionType

S = 2048
D = 2048
FF = 4096
NSC = 16
NQ = 4
SCQ = 4
QT = 512

EPS_LN = 1e-5
EPS_ATTN = 1e-6
T1CLIP = float(np.float32(1e-5 * math.sqrt(2048.0)))
T2CLIP = float(np.float32(1e-5 * 64.0))

_s1 = np.arange(2048)
PERM_F1 = ((_s1 & 15) << 7) | ((_s1 >> 7) << 3) | ((_s1 >> 4) & 7)
_s2 = np.arange(4096)
PERM_F2 = ((_s2 & 31) << 7) | ((_s2 >> 7) << 2) | ((_s2 >> 5) & 3)

ENGINES = ("tensor", "vector", "scalar", "gpsimd", "sync")
SEMS = ("mm", "ev", "ac", "gp", "hb", "dp0", "dp1", "ds0", "ds1", "dgl",
        "dst", "dbk", "dwkv", "dg0", "dg1", "dbq")
COMPUTE_SEMS = ("mm", "ev", "ac", "gp")


class KB:
    def __init__(self, nc):
        self.nc = nc
        self.steps = {e: [] for e in ENGINES}
        self.sems = {s: nc.ctx.enter_context(nc.semaphore(s)) for s in SEMS}
        self.counts = {s: 0 for s in SEMS}
        self._stage_pending = {}

    def val(self, s):
        return self.counts[s]

    def snapshot(self):
        return {s: self.counts[s] for s in COMPUTE_SEMS}

    def stage(self, engines=ENGINES):
        snap = self.snapshot()
        for e in engines:
            self._stage_pending[e] = snap

    def op(self, eng, emit, inc=None, waits=()):
        for w, v in waits:
            assert v <= self.counts[w], f"wait {w}={v} > {self.counts[w]}"
        pend = self._stage_pending.pop(eng, None)
        if inc is not None:
            self.counts[inc[0]] += inc[1]
        sems = self.sems

        def thunk(eobj, _emit=emit, _waits=list(waits), _inc=inc, _pend=pend):
            if _pend:
                for s, v in _pend.items():
                    if v > 0:
                        eobj.wait_ge(sems[s], v)
            for w, v in _waits:
                if v > 0:
                    eobj.wait_ge(sems[w], v)
            ins = _emit(eobj)
            if _inc is not None:
                assert ins is not None
                ins.then_inc(sems[_inc[0]], _inc[1])
            return ins
        self.steps[eng].append(thunk)


def build(debug_taps=(), stop_after=None, TAPQ=0):
    nc = bass.Bass("TRN2", target_bir_lowering=False, debug=False)
    kb = KB(nc)
    taps = set(debug_taps)

    x_in = nc.dram_tensor("x_in", [S, D], F32, kind="ExternalInput").ap()
    wqT = nc.dram_tensor("wqT", [D, D], F32, kind="ExternalInput").ap()
    wkT_d = nc.dram_tensor("wkT", [D, 512], F32, kind="ExternalInput").ap()
    wvT_d = nc.dram_tensor("wvT", [D, 512], F32, kind="ExternalInput").ap()
    woT = nc.dram_tensor("woT", [D, D], F32, kind="ExternalInput").ap()
    wupT = nc.dram_tensor("wupT", [D, FF], F32, kind="ExternalInput").ap()
    wdnT = nc.dram_tensor("wdnT", [FF, D], F32, kind="ExternalInput").ap()
    bq_in = nc.dram_tensor("bq_in", [128, 16], F32, kind="ExternalInput").ap()
    bk_in = nc.dram_tensor("bk_in", [1, 512], F32, kind="ExternalInput").ap()
    bv_in = nc.dram_tensor("bv_in", [1, 512], F32, kind="ExternalInput").ap()
    y_out = nc.dram_tensor("y_out", [S, D], F32, kind="ExternalOutput").ap()
    kind_s = "ExternalOutput" if "prep" in taps else "Internal"
    wq_bf = nc.dram_tensor("wq_bf", [D, D], BF16, kind=kind_s).ap()
    wo_bf = nc.dram_tensor("wo_bf", [D, D], BF16, kind=kind_s).ap()
    wup_t = nc.dram_tensor("wup_t", [D, FF], BF16, kind=kind_s).ap()
    wdn_t = nc.dram_tensor("wdn_t", [FF, D], BF16, kind=kind_s).ap()

    tap_t = {}

    def tap(name, shape, dtype=F32):
        if name in taps:
            if name not in tap_t:
                tap_t[name] = nc.dram_tensor("tap_" + name, shape, dtype,
                                             kind="ExternalOutput").ap()
            return tap_t[name]
        return None

    def emit_tap(name, src_ap):
        t = tap_t.get(name)
        if t is not None:
            kb.stage(("sync",))
            kb.op("sync", lambda s, t=t, src_ap=src_ap: s.dma_start(
                out=t, in_=src_ap), inc=("dst", 16))
            wt_ = kb.val("dst")
            kb.op("vector", lambda v: v.engine_nop(), inc=("ev", 1),
                  waits=[("dst", wt_)])

    es = nc.ctx

    def sb(name, shape, dtype):
        return es.enter_context(nc.sbuf_tensor(name, shape, dtype))

    cst = sb("cst", [128, 8], F32)
    ones_bf = sb("ones_bf", [128, 128], BF16)
    ones_f32 = sb("ones_f32", [128, 128], F32)
    ident = sb("ident", [128, 128], BF16)
    h128 = sb("h128", [128, 128], BF16)
    hB1 = sb("hB1", [128, 128], BF16)
    hB2 = sb("hB2", [128, 128], BF16)
    negscr = sb("negscr", [128, 128], BF16)
    bq_sb = sb("bq_sb", [128, 16], F32)
    stats = sb("stats", [128, 32], F32)
    kv_sb = sb("kv_sb", [128, 512], BF16)
    ks_sb = sb("ks_sb", [128, 4], BF16)
    wsum = sb("wsum", [128, 2], F32)
    wsc = sb("wsc", [128, 4], F32)
    r_sb = sb("r_sb", [128, 64], F32)
    a1c_sb = sb("a1c_sb", [128, 4], F32)
    fs_sb = sb("fs_sb", [128, 4], F32)
    st_ = sb("sstat", [128, 16], F32)
    ln2st = sb("ln2st", [128, 8], F32)

    x_q = sb("x_q", [128, SCQ * D], F32)
    arena = sb("arena", [128, 32768], BF16)
    a2b = sb("a2b", [128, 16384], BF16)
    h1f = sb("h1f", [128, 16 * QT], BF16)
    sq_q = sb("sq_q", [128, 16 * QT], BF16)
    scr_bf = sb("scr_bf", [128, FF], BF16)
    h2tm = scr_bf[:, 0:2048]
    h2f = scr_bf[:, 2048:4096]
    pa2 = scr_bf[:, :]
    pa1 = sb("pa1", [128, D], BF16)
    y2f = sb("y2f", [128, FF], F32)
    xsc = y2f[:, 2048:4096]
    q2b = sb("q2b", [128, FF], BF16)
    qi32 = sb("qi32", [128, 1024], I32)
    tqf = sb("tqf", [128, 1024], F32)

    a2f = x_q
    corr = h1f[:, 2048:3072]
    cdq = h1f[:, 3072:4096]
    ltqs = h1f[:, 0:1024]
    lqbf = h1f[:, 1024:2048]
    bk_sb = arena[0:1, 16384:16896]
    bv_sb = arena[0:1, 16896:17408]
    qkp = h1f
    q1b = q2b
    skv = sq_q
    banks = [es.enter_context(nc.psum_tensor(f"pb{i}", [128, 512], F32))
             for i in range(8)]

    bank_ev = {i: [] for i in range(8)}

    def elu_tiles_b(p):
        base = p * 2048
        return (h1f[:, base:base + 512], h1f[:, base + 512:base + 1024],
                h1f[:, base + 1024:base + 1536])

    def elu_tiles(p):
        if p == 0:
            return tqf[:, 0:512], tqf[:, 512:1024], y2f[:, 0:512]
        return y2f[:, 1024:1536], y2f[:, 1536:2048], y2f[:, 512:1024]

    def wk_tile(dc):
        return arena[:, dc * 1024:dc * 1024 + 512]

    def wv_tile(dc):
        return arena[:, dc * 1024 + 512:(dc + 1) * 1024]

    def wq_tile(dc):
        return arena[:, dc * 2048:(dc + 1) * 2048]

    def gq_sl(fc, lo, hi):
        return arena[:, fc * 512 + lo:fc * 512 + hi]

    def q1f_sl(dc, lo, hi):
        return arena[:, 16384 + dc * 512 + lo:16384 + dc * 512 + hi]

    def g_seeds1(g):
        for col, v in enumerate((EPS_LN, -0.5, 128.0, 127.5, T1CLIP, T2CLIP,
                                 1.0, 128.5)):
            g.memset(cst[:, col:col + 1], v)
        g.memset(ones_bf[:], 1.0)
        g.memset(ones_f32[:], 1.0)
        g.memset(ident[:], 1.0)
        for h in (h128, hB1, hB2):
            g.memset(h[:], 0.0)
        g.memset(wsum[:], 0.0)
        return g.memset(negscr[:], 0.0)
    kb.op("gpsimd", g_seeds1, inc=("gp", 1))

    def g_seeds2(g):
        for h in (h128, hB1, hB2):
            g.memset(h[0:1, 0:1], 1.0)
        return g.affine_select(ident[:], ident[:], pattern=[[-1, 128]],
                               compare_op=ALU.is_equal, fill=0.0,
                               channel_multiplier=1)
    kb.op("gpsimd", g_seeds2, inc=("gp", 1), waits=[("gp", 1)])
    kb.op("sync", lambda s: s.dma_start(out=bq_sb[:], in_=bq_in), inc=("dbq", 16))
    kb.op("gpsimd", lambda g: g.dma_start(out=bk_sb[:], in_=bk_in), inc=("dbk", 16))
    kb.op("gpsimd", lambda g: g.dma_start(out=bv_sb[:], in_=bv_in), inc=("dbk", 16))

    kb.stage(("vector", "scalar"))
    jobs = []
    k = 1
    while k < 128:
        jobs.append(("dbl", h128, k)); k *= 2
    k = 1
    while k < 16:
        jobs.append(("dbl", hB1, k)); k *= 2
    k = 16
    while k < 128:
        jobs.append(("rep", hB1, k)); k *= 2
    k = 1
    while k < 32:
        jobs.append(("dbl", hB2, k)); k *= 2
    k = 32
    while k < 128:
        jobs.append(("rep", hB2, k)); k *= 2
    for kind, ht, k in jobs:
        if kind == "dbl":
            kb.op("vector",
                  lambda v, ht=ht, k=k: v.tensor_copy(out=ht[0:k, k:2 * k],
                                                      in_=ht[0:k, 0:k]),
                  waits=[("gp", 2), ("hb", kb.val("hb"))])
            kb.op("vector",
                  lambda v, ht=ht, k=k: v.tensor_scalar(
                      out=negscr[0:k, 0:k], in0=ht[0:k, 0:k], scalar1=-1.0,
                      scalar2=None, op0=ALU.mult), inc=("hb", 1))
            w = kb.val("hb")
            kb.op("scalar", lambda sc, ht=ht, k=k: sc.dma_start(
                out=ht[k:2 * k, 0:k], in_=ht[0:k, 0:k]),
                inc=("hb", 16), waits=[("hb", w)])
            kb.op("scalar", lambda sc, ht=ht, k=k: sc.dma_start(
                out=ht[k:2 * k, k:2 * k], in_=negscr[0:k, 0:k]), inc=("hb", 16))
        else:
            w = kb.val("hb")
            kb.op("scalar", lambda sc, ht=ht, k=k: sc.dma_start(
                out=ht[k:2 * k, k:2 * k], in_=ht[0:k, 0:k]),
                inc=("hb", 16), waits=[("hb", w)])
    HB_DONE = kb.val("hb")

    def quant_tail(width, slot, out_ap, wq0, extra_final=()):
        n = width
        o = slot * n
        qs = qi32[:, o:o + n]
        ts = tqf[:, o:o + n]
        l_corr = corr[:, o:o + n]
        l_cd = cdq[:, o:o + n]
        l_up = ltqs[:, o:o + n]
        l_dn = lqbf[:, o:o + n]
        kb.op("vector", lambda v: v.tensor_tensor(
            out=l_corr, in0=ts, in1=qs,
            op=ALU.is_ge), inc=("ev", 1), waits=[("ac", wq0)])
        kb.op("vector", lambda v: v.scalar_tensor_tensor(
            out=l_cd, in0=qs, scalar=-1.0,
            in1=ts, op0=ALU.add, op1=ALU.is_gt), inc=("ev", 1))
        kb.op("vector", lambda v: v.scalar_tensor_tensor(
            out=l_up, in0=qs, scalar=1.0,
            in1=ts, op0=ALU.add, op1=ALU.is_le), inc=("ev", 1))
        kb.op("vector", lambda v: v.scalar_tensor_tensor(
            out=l_dn, in0=qs, scalar=-2.0,
            in1=ts, op0=ALU.add, op1=ALU.is_gt), inc=("ev", 1))
        w4 = kb.val("ev")
        kb.op("vector", lambda v: v.tensor_tensor(
            out=l_corr, in0=l_corr, in1=l_cd,
            op=ALU.subtract), inc=("ev", 1), waits=[("ev", w4)])
        kb.op("vector", lambda v: v.tensor_tensor(
            out=l_up, in0=l_up, in1=l_dn,
            op=ALU.subtract), inc=("ev", 1))
        w6 = kb.val("ev")
        kb.op("vector", lambda v: v.tensor_tensor(
            out=l_corr, in0=l_corr, in1=l_up,
            op=ALU.add), inc=("ev", 1), waits=[("ev", w6)])
        wcn = kb.val("ev")
        kb.op("vector", lambda v, out_ap=out_ap: v.scalar_tensor_tensor(
            out=out_ap, in0=qs, scalar=-128.0,
            in1=l_corr, op0=ALU.add, op1=ALU.add),
            inc=("ev", 1), waits=[("ev", wcn)] + list(extra_final))

    def ln_stats(x_ap, out0, out1, first_waits=(), scratch=None):
        scr = y2f[:, 0:2048] if scratch is None else scratch
        kb.op("scalar", lambda s_, scr=scr: s_.activation(
            out=scr, in_=x_ap, func=ACTF.Identity,
            accum_out=st_[:, 0:1]), inc=("ac", 1), waits=list(first_waits))
        w0 = kb.val("ac")
        kb.op("scalar", lambda s_, scr=scr: s_.activation(
            out=scr, in_=x_ap, func=ACTF.Square,
            accum_out=st_[:, 1:2]), inc=("ac", 1), waits=[("ac", w0)])
        w1 = kb.val("ac")

        def chain1(v):
            v.tensor_scalar(out=st_[:, 2:3], in0=st_[:, 0:1],
                            scalar1=1.0 / 2048, scalar2=None, op0=ALU.mult)
            return v.tensor_scalar(out=st_[:, 3:4], in0=st_[:, 1:2],
                                   scalar1=1.0 / 2048, scalar2=None, op0=ALU.mult)
        kb.op("vector", chain1, inc=("ev", 1), waits=[("ac", w1)])
        w2 = kb.val("ev")
        kb.op("vector", lambda v: v.tensor_tensor(
            out=st_[:, 4:5], in0=st_[:, 2:3], in1=st_[:, 2:3], op=ALU.mult),
            inc=("ev", 1), waits=[("ev", w2)])
        w3 = kb.val("ev")
        kb.op("vector", lambda v: v.tensor_tensor(
            out=st_[:, 5:6], in0=st_[:, 3:4], in1=st_[:, 4:5], op=ALU.subtract),
            inc=("ev", 1), waits=[("ev", w3)])
        w4 = kb.val("ev")
        kb.op("scalar", lambda s_: s_.activation(
            out=st_[:, 6:7], in_=st_[:, 5:6], func=ACTF.Ln, bias=cst[:, 0:1],
            scale=1.0), inc=("ac", 1), waits=[("ev", w4)])
        w5 = kb.val("ac")
        kb.op("scalar", lambda s_: s_.activation(
            out=out0, in_=st_[:, 6:7], func=ACTF.Exp, bias=0.0,
            scale=cst[:, 1:2]), inc=("ac", 1), waits=[("ac", w5)])
        w6 = kb.val("ac")
        kb.op("vector", lambda v: v.scalar_tensor_tensor(
            out=out1, in0=st_[:, 2:3], scalar=-1.0, in1=out0,
            op0=ALU.mult, op1=ALU.mult), inc=("ev", 1), waits=[("ac", w6)])

    def transpose_pass(n, src_sl, dst4, bank_base=0, rhs=None, first_waits=(),
                       evac="vector", nbanks=2):
        rhs = ident[:] if rhs is None else rhs
        assert n % 4 == 0
        last = {}
        blist = ([bank_base, bank_base + 1] if nbanks == 2 else [0, 1, 6, 7])
        for g in range(n // 4):
            bi = blist[g % len(blist)]
            bk_ = banks[bi]
            w0 = list(bank_ev[bi]) + (list(first_waits) if g == 0 else [])
            for t4 in range(4):
                t = g * 4 + t4
                kb.op("tensor", lambda te, t=t, t4=t4, bk_=bk_, rhs=rhs: te.matmul(
                    bk_[0:128, t4 * 128:(t4 + 1) * 128], src_sl(t), rhs,
                    start=True, stop=True),
                    inc=("mm", 1) if t4 == 3 else None,
                    waits=w0 if t4 == 0 else [])
            wm = kb.val("mm")
            out_ap, in_ap = dst4(g, bk_)
            eng = evac if evac != "alt" else ("vector" if g % 2 == 0 else "scalar")
            if eng == "vector":
                kb.op("vector", lambda v, out_ap=out_ap, in_ap=in_ap: v.tensor_copy(
                    out=out_ap, in_=in_ap), inc=("ev", 1), waits=[("mm", wm)])
                bank_ev[bi] = [("ev", kb.val("ev"))]
                last["ev"] = kb.val("ev")
            else:
                kb.op("scalar", lambda s_, out_ap=out_ap, in_ap=in_ap: s_.activation(
                    out=out_ap, in_=in_ap, func=ACTF.Identity),
                    inc=("ac", 1), waits=[("mm", wm)])
                bank_ev[bi] = [("ac", kb.val("ac"))]
                last["ac"] = kb.val("ac")
        return [(s, v) for s, v in last.items()]

    def d4_contig(buf, base=0):
        return lambda g, bk: (buf[:, base + g * 512:base + (g + 1) * 512],
                              bk[0:128, 0:512])

    def d4_strided(ten_ap, base, ppitch):
        def f(g, bk):
            return (bass.AP(ten_ap.tensor, base + g * 4 * 512,
                            [[ppitch, 128], [512, 4], [1, 128]]),
                    bass.AP(bk[:].tensor, 0, [[512, 128], [128, 4], [1, 128]]))
        return f

    def d4_inter(ten_ap, nch, ppitch):
        def f(g, bk):
            return (bass.AP(ten_ap.tensor, g * 4,
                            [[ppitch, 128], [nch, 128], [1, 4]]),
                    bass.AP(bk[:].tensor, 0, [[512, 128], [1, 128], [128, 4]]))
        return f

    def _finish():
        with nc.Block() as block:
            block.tensor(lambda e: [f(e) for f in kb.steps["tensor"]])
            block.vector(lambda e: [f(e) for f in kb.steps["vector"]])
            block.scalar(lambda e: [f(e) for f in kb.steps["scalar"]])
            block.gpsimd(lambda e: [f(e) for f in kb.steps["gpsimd"]])
            block.sync(lambda e: [f(e) for f in kb.steps["sync"]])
        return nc, sorted(tap_t)

    bst = {"war0": [], "war1": [], "h2tm_war_mm": 0, "exp_ac": 0,
           "skv_war": 0, "next": 0}

    def emit_b_chunk():
        sc = bst["next"]
        if sc >= 16:
            return
        bst["next"] = sc + 1
        sl = sc % 2
        xs = y2f[:, 2048:4096] if sl == 0 else y2f[:, 0:2048]
        kb.op("sync", lambda s, sc=sc, xs=xs: s.dma_start(
            out=xs, in_=x_in[sc * 128:(sc + 1) * 128, :]), inc=("dgl", 16),
            waits=bst["war0" if sl == 0 else "war1"])
        wl = kb.val("dgl")
        ln_stats(xs, stats[:, 2 * sc:2 * sc + 1],
                 stats[:, 2 * sc + 1:2 * sc + 2],
                 first_waits=[("dgl", wl)], scratch=a2b[:, 12288:14336])
        wst = kb.val("ev")
        kb.op("scalar", lambda s_, sc=sc, xs=xs: s_.activation(
            out=h2tm[:], in_=xs, func=ACTF.Identity,
            scale=stats[:, 2 * sc:2 * sc + 1],
            bias=stats[:, 2 * sc + 1:2 * sc + 2]),
            inc=("ac", 1), waits=[("ev", wst), ("mm", bst["h2tm_war_mm"])])
        wh1 = kb.val("ac")
        bst["war0" if sl == 0 else "war1"] = [("ac", wh1)]
        fw = [("ac", wh1)]
        if sc == 0:
            fw += [("hb", HB_DONE), ("gp", 2)]
        transpose_pass(16, lambda t: h2tm[:, t * 128:(t + 1) * 128],
                       d4_contig(scr_bf, base=2048),
                       bank_base=0, first_waits=fw, nbanks=4)
        bst["h2tm_war_mm"] = kb.val("mm")
        wh1f = kb.val("ev")
        for which, bi, wt, brow in (("k", 2, wk_tile, bk_sb), ("v", 3, wv_tile, bv_sb)):
            w0 = [("ev", wh1f), ("dwkv", DWKV)] + bank_ev[bi]
            for dc in range(16):
                kb.op("tensor", lambda te, dc=dc, bi=bi, wt=wt: te.matmul(
                    banks[bi][0:128, 0:512], h2f[:, dc * 128:(dc + 1) * 128],
                    wt(dc), start=(dc == 0), stop=False),
                    waits=w0 if dc == 0 else [])
            kb.op("tensor", lambda te, bi=bi, brow=brow: te.matmul(
                banks[bi][0:128, 0:512], ones_bf[0:1, 0:128], brow[0:1, :],
                start=False, stop=True), inc=("mm", 1), waits=[("dbk", 32)])
        wkvm = kb.val("mm")
        eu, ee, er = elu_tiles_b(sc % 2)
        kb.op("vector", lambda v, eu=eu: v.tensor_single_scalar(
            out=eu, in_=banks[2][0:128, 0:512], scalar=0.0, op=ALU.min),
            inc=("ev", 1), waits=[("mm", wkvm), ("ac", bst["exp_ac"])])
        wu = kb.val("ev")
        kb.op("scalar", lambda s_, er=er: s_.activation(
            out=er, in_=banks[2][0:128, 0:512], func=ACTF.Relu),
            inc=("ac", 1), waits=[("mm", wkvm), ("ev", wu)])
        wrel = kb.val("ac")
        kb.op("scalar", lambda s_, eu=eu, ee=ee: s_.activation(
            out=ee, in_=eu, func=ACTF.Exp), inc=("ac", 1),
            waits=[("ev", wu)])
        wee = kb.val("ac")
        bst["exp_ac"] = wee
        bank_ev[2] = [("ev", wu), ("ac", wrel)]
        kb.op("vector", lambda v, ee=ee, er=er: v.tensor_tensor(
            out=skv[:, 0:512], in0=ee, in1=er, op=ALU.add),
            inc=("ev", 1), waits=[("ac", wee), ("mm", bst["skv_war"])])
        kb.op("vector", lambda v: v.tensor_copy(
            out=skv[:, 512:1024], in_=banks[3][0:128, 0:512]), inc=("ev", 1),
            waits=[("mm", wkvm), ("mm", bst["skv_war"])])
        wsk = kb.val("ev")
        bank_ev[3] = [("ev", wsk)]
        for kh in range(4):
            kb.op("tensor", lambda te, kh=kh, sc=sc: te.matmul(
                banks[4][0:128, kh * 128:(kh + 1) * 128],
                skv[:, kh * 128:(kh + 1) * 128],
                skv[:, 512 + kh * 128:512 + (kh + 1) * 128],
                start=(sc == 0 and kh == 0), stop=(sc == NSC - 1 and kh == 3)),
                inc=("mm", 1) if (sc == NSC - 1 and kh == 3) else None,
                waits=[("ev", wsk)] if kh == 0 else [])
        for kh in range(4):
            kb.op("tensor", lambda te, kh=kh, sc=sc: te.matmul(
                banks[5][0:128, kh:kh + 1],
                skv[:, kh * 128:(kh + 1) * 128], ones_bf[:, 0:1],
                start=(sc == 0 and kh == 0), stop=(sc == NSC - 1 and kh == 3)),
                inc=("mm", 1))
        bst["skv_war"] = kb.val("mm")


    for dc in range(16):
        kb.op("gpsimd", lambda g, dc=dc: g.dma_start(
            out=wk_tile(dc), in_=wkT_d[dc * 128:(dc + 1) * 128, :]), inc=("dwkv", 16))
        kb.op("gpsimd", lambda g, dc=dc: g.dma_start(
            out=wv_tile(dc), in_=wvT_d[dc * 128:(dc + 1) * 128, :]), inc=("dwkv", 16))
    DWKV = kb.val("dwkv")

    for src, dst in ((wqT, wq_bf), (woT, wo_bf)):
        for dc in range(16):
            par = dc % 2
            dp, ds = ("dg0", "ds0") if par == 0 else ("dg1", "ds1")
            ap = a2b[:, par * 2048:(par + 1) * 2048]
            wstore = kb.val(ds)
            kb.op("gpsimd", lambda g, ap=ap, src=src, dc=dc: g.dma_start(
                out=ap, in_=src[dc * 128:(dc + 1) * 128, :]),
                inc=(dp, 16), waits=[(ds, wstore)])
            wload = kb.val(dp)
            kb.op("scalar", lambda sc, ap=ap, dst=dst, dc=dc: sc.dma_start(
                out=dst[dc * 128:(dc + 1) * 128, :], in_=ap),
                inc=(ds, 16), waits=[(dp, wload)])
    DS_WQO = (kb.val("ds0"), kb.val("ds1"))

    sum_ac = {0: 0, 1: 0}
    sum_ev = 0
    for wi, (src_, rows, ncols) in enumerate(((wupT, D, FF), (wdnT, FF, D))):
        for t in range(rows // 128):
            par = t % 2
            dp = "dp0" if par == 0 else "dp1"
            ap = a2f[:, par * 4096:par * 4096 + ncols]
            kb.op("sync", lambda s, ap=ap, src_=src_, t=t: s.dma_start(
                out=ap, in_=src_[t * 128:(t + 1) * 128, :]),
                inc=(dp, 16), waits=[("ac", sum_ac[par])])
            wl = kb.val(dp)
            kb.op("scalar", lambda sc, ap=ap, ncols=ncols: sc.activation(
                out=arena[:, 32768 - ncols:32768], in_=ap, func=ACTF.Abs,
                accum_out=st_[:, 8:9]), inc=("ac", 1),
                waits=[(dp, wl), ("ev", sum_ev)])
            wa = kb.val("ac")
            sum_ac[par] = wa
            kb.op("vector", lambda v, wi=wi: v.tensor_tensor(
                out=wsum[:, wi:wi + 1], in0=wsum[:, wi:wi + 1],
                in1=st_[:, 8:9], op=ALU.add), inc=("ev", 1),
                waits=[("ac", wa), ("ev", sum_ev), ("gp", 2)])
            sum_ev = kb.val("ev")
            if (wi * 16 + t) % 5 == 4:
                emit_b_chunk()
    kb.op("tensor", lambda te: te.matmul(
        banks[7][0:128, 0:1], ones_f32[:], wsum[:, 0:1], start=True, stop=False),
        waits=[("ev", kb.val("ev")), ("gp", 2)])
    kb.op("tensor", lambda te: te.matmul(
        banks[7][0:128, 1:2], ones_f32[:], wsum[:, 1:2], start=False, stop=True),
        inc=("mm", 1))
    MM_WSUM = kb.val("mm")
    kb.op("vector", lambda v: v.tensor_scalar(
        out=wsc[:, 0:1], in0=banks[7][0:128, 0:1], scalar1=1.0 / (D * FF),
        scalar2=EPS_LN, op0=ALU.mult, op1=ALU.max), inc=("ev", 1),
        waits=[("mm", MM_WSUM)])
    kb.op("vector", lambda v: v.tensor_scalar(
        out=wsc[:, 2:3], in0=banks[7][0:128, 1:2], scalar1=1.0 / (D * FF),
        scalar2=EPS_LN, op0=ALU.mult, op1=ALU.max), inc=("ev", 1))
    w = kb.val("ev")
    bank_ev[7] = [("ev", w)]
    kb.op("vector", lambda v: v.reciprocal(out=wsc[:, 1:2], in_=wsc[:, 0:1]),
          inc=("ev", 1), waits=[("ev", w)])
    kb.op("vector", lambda v: v.reciprocal(out=wsc[:, 3:4], in_=wsc[:, 2:3]),
          inc=("ev", 1))
    EV_WSC = kb.val("ev")
    tap("wsc", [128, 4]); emit_tap("wsc", wsc[:])

    DSB = (kb.val("ds0"), kb.val("ds1"))
    t_lad = {0: kb.val("ev"), 1: kb.val("ev")}
    fap_ac = {0: kb.val("ac"), 1: kb.val("ac")}
    pc = [0]
    for wi, (src_, dst, rows, cols, invcol) in enumerate(
            ((wupT, wup_t, D, FF, 1), (wdnT, wdn_t, FF, D, 3))):
        for t in range(rows // 128):
            par = t % 2
            dp, ds = ("dp0", "ds0") if par == 0 else ("dp1", "ds1")
            fap = a2f[:, par * 4096:par * 4096 + cols]
            bap = a2b[:, par * 8192:par * 8192 + cols]
            wprev_b = kb.val(ds)
            kb.op("sync", lambda s, fap=fap, src_=src_, t=t: s.dma_start(
                out=fap, in_=src_[t * 128:(t + 1) * 128, :]),
                inc=(dp, 16), waits=[("ac", fap_ac[par])])
            wl = kb.val(dp)
            for sub in range(cols // 512):
                slot = pc[0] % 2
                pc[0] += 1
                ts_ = tqf[:, slot * 512:(slot + 1) * 512]
                cr = pa1[:, slot * 512:slot * 512 + 512]
                cd_ = pa1[:, 1024 + slot * 512:1024 + slot * 512 + 512]
                fsub = a2f[:, par * 4096 + sub * 512:par * 4096 + (sub + 1) * 512]
                bsub = a2b[:, par * 8192 + sub * 512:par * 8192 + (sub + 1) * 512]
                kb.op("scalar", lambda sc, fsub=fsub, invcol=invcol, ts_=ts_: sc.activation(
                    out=ts_, in_=fsub, func=ACTF.Identity,
                    scale=wsc[:, invcol:invcol + 1]),
                    inc=("ac", 1),
                    waits=[(dp, wl), ("ev", max(EV_WSC, t_lad[slot])),
                           ("gp", kb.val("gp"))])
                wa = kb.val("ac")
                kb.op("vector", lambda v, cr=cr, ts_=ts_: v.tensor_single_scalar(
                    out=cr, in_=ts_, scalar=0.5, op=ALU.is_ge),
                    inc=("ev", 1), waits=[("ac", wa)])
                kb.op("vector", lambda v, cd_=cd_, ts_=ts_: v.tensor_single_scalar(
                    out=cd_, in_=ts_, scalar=-0.5, op=ALU.is_le),
                    inc=("ev", 1))
                wcd = kb.val("ev")
                kb.op("vector", lambda v, bsub=bsub, cr=cr, cd_=cd_: v.tensor_tensor(
                    out=bsub, in0=cr, in1=cd_, op=ALU.subtract), inc=("ev", 1),
                    waits=[("ev", wcd), (ds, wprev_b),
                           ("ds0", DSB[0]), ("ds1", DSB[1])])
                t_lad[slot] = kb.val("ev")
                fap_ac[par] = kb.val("ac")
            we = kb.val("ev")
            kb.op("scalar", lambda sc, bap=bap, dst=dst, t=t: sc.dma_start(
                out=dst[t * 128:(t + 1) * 128, :], in_=bap),
                inc=(ds, 16), waits=[("ev", we)])
            if (wi * 16 + t) % 6 == 5:
                emit_b_chunk()
    DS_TERN = (kb.val("ds0"), kb.val("ds1"))
    AC_TERN = kb.val("ac")

    if stop_after == "prep":
        return _finish()

    while bst["next"] < 16:
        emit_b_chunk()
    MM_KV = kb.val("mm")
    kb.op("vector", lambda v: v.tensor_copy(out=kv_sb[:], in_=banks[4][0:128, 0:512]),
          inc=("ev", 1), waits=[("mm", MM_KV)])
    kb.op("vector", lambda v: v.tensor_copy(out=ks_sb[:], in_=banks[5][0:128, 0:4]),
          inc=("ev", 1))
    bank_ev[4] = [("ev", kb.val("ev"))]
    bank_ev[5] = [("ev", kb.val("ev"))]
    tap("kv", [128, 512], BF16); emit_tap("kv", kv_sb[:])
    tap("ks", [128, 4], BF16); emit_tap("ks", ks_sb[:])
    tap("stats", [128, 32]); emit_tap("stats", stats[:])

    if stop_after == "B":
        return _finish()

    arena_war_mm = MM_KV
    a2b_slot_mm = {0: 0, 1: 0}
    for q in range(NQ):
        sc0 = q * SCQ
        DGL_PREV = kb.val("dgl")
        DST_PREV = kb.val("dst")
        for i in range(SCQ):
            kb.op("sync", lambda s, i=i, sc0=sc0: s.dma_start(
                out=x_q[:, i * D:(i + 1) * D],
                in_=x_in[(sc0 + i) * 128:(sc0 + i + 1) * 128, :]),
                inc=("dgl", 16),
                waits=[("dgl", DGL_PREV), ("dst", DST_PREV)] + (
                    [("ac", AC_TERN)] if q == 0 else []))
        for dc in range(16):
            kb.op("sync", lambda s, dc=dc: s.dma_start(
                out=wq_tile(dc), in_=wq_bf[dc * 128:(dc + 1) * 128, :]),
                inc=("dgl", 16),
                waits=[("dgl", DGL_PREV), ("mm", arena_war_mm),
                       ("ds0", DS_WQO[0]), ("ds1", DS_WQO[1])] + (
                    [("ac", AC_TERN)] if q == 0 else []))
        DLS_XWQ = kb.val("dgl")

        h2tm_war_mm = kb.val("mm")
        for i in range(SCQ):
            sc = sc0 + i
            kb.op("scalar", lambda s_, i=i, sc=sc: s_.activation(
                out=h2tm[:], in_=x_q[:, i * D:(i + 1) * D], func=ACTF.Identity,
                scale=stats[:, 2 * sc:2 * sc + 1],
                bias=stats[:, 2 * sc + 1:2 * sc + 2]), inc=("ac", 1),
                waits=[("dgl", DLS_XWQ), ("mm", h2tm_war_mm)])
            wh = kb.val("ac")
            transpose_pass(16, lambda t: h2tm[:, t * 128:(t + 1) * 128],
                           d4_strided(h1f[:], i * 128, 16 * QT),
                           bank_base=0, first_waits=[("ac", wh)], nbanks=4)
            h2tm_war_mm = kb.val("mm")
        EV_H1F = kb.val("ev")

        elu_ac = {0: 0, 1: 0}
        elu_ev = {0: 0, 1: 0}
        for hc in range(16):
            p = hc % 2
            bnk = banks[2 + p]
            w0 = [("ev", EV_H1F), ("dgl", DLS_XWQ)] + bank_ev[2 + p]
            for dc in range(16):
                kb.op("tensor", lambda te, hc=hc, dc=dc, bnk=bnk: te.matmul(
                    bnk[0:128, 0:512], wq_tile(dc)[:, hc * 128:(hc + 1) * 128],
                    h1f[:, dc * 512:(dc + 1) * 512],
                    start=(dc == 0), stop=(dc == 15)),
                    inc=("mm", 1) if dc == 15 else None,
                    waits=w0 if dc == 0 else [])
            wq_mm = kb.val("mm")
            eu, ee, er = elu_tiles(p)
            kb.op("vector", lambda v, hc=hc, bnk=bnk, eu=eu: v.tensor_scalar(
                out=eu, in0=bnk[0:128, 0:512], scalar1=bq_sb[:, hc:hc + 1],
                scalar2=0.0, op0=ALU.add, op1=ALU.min), inc=("ev", 1),
                waits=[("mm", wq_mm), ("dbq", 16), ("ac", elu_ac[p])])
            wu = kb.val("ev")
            kb.op("scalar", lambda s_, hc=hc, bnk=bnk, er=er: s_.activation(
                out=er, in_=bnk[0:128, 0:512], func=ACTF.Relu,
                bias=bq_sb[:, hc:hc + 1], scale=1.0), inc=("ac", 1),
                waits=[("mm", wq_mm), ("ev", wu), ("ev", elu_ev[p])])
            wrel = kb.val("ac")
            kb.op("scalar", lambda s_, eu=eu, ee=ee: s_.activation(
                out=ee, in_=eu, func=ACTF.Exp), inc=("ac", 1))
            wee = kb.val("ac")
            elu_ac[p] = wee
            bank_ev[2 + p] = [("ev", wu), ("ac", wrel)]
            kb.op("vector", lambda v, hc=hc, ee=ee, er=er: v.tensor_tensor(
                out=sq_q[:, hc * 512:(hc + 1) * 512], in0=ee, in1=er,
                op=ALU.add), inc=("ev", 1), waits=[("ac", wee)])
            elu_ev[p] = kb.val("ev")
        EV_SQ = kb.val("ev")
        if q == TAPQ:
            tap("sq", [128, 16 * QT], BF16); emit_tap("sq", sq_q[:])

        w0 = [("ev", EV_SQ)] + bank_ev[4]
        for sl in range(4):
            for hc in range(16):
                kb.op("tensor", lambda te, sl=sl, hc=hc: te.matmul(
                    banks[4][0:128, sl * 16 + hc:sl * 16 + hc + 1],
                    sq_q[:, hc * 512 + sl * 128:hc * 512 + (sl + 1) * 128],
                    ks_sb[:, hc // 4:hc // 4 + 1], start=True, stop=True),
                    inc=("mm", 1) if (sl == 3 and hc == 15) else None,
                    waits=w0 if (sl == 0 and hc == 0) else [])
        MM_NRM = kb.val("mm")
        kb.op("vector", lambda v: v.tensor_scalar(
            out=r_sb[:], in0=banks[4][0:128, 0:64], scalar1=EPS_ATTN,
            scalar2=None, op0=ALU.add), inc=("ev", 1), waits=[("mm", MM_NRM)])
        wr = kb.val("ev")
        bank_ev[4] = [("ev", wr)]
        kb.op("vector", lambda v: v.reciprocal(out=r_sb[:], in_=r_sb[:]),
              inc=("ev", 1), waits=[("ev", wr)])
        EV_R = kb.val("ev")
        if q == TAPQ:
            tap("r", [128, 64]); emit_tap("r", r_sb[:])

        q2b_war_mm = kb.val("mm")
        for n_ in range(16):
            sl, hq = n_ // 4, n_ % 4
            p = n_ % 2
            bnk = banks[2 + p]
            w0 = [("ev", EV_SQ)] + bank_ev[2 + p]
            for hh in range(4):
                hc = hq * 4 + hh
                kb.op("tensor", lambda te, sl=sl, hc=hc, hh=hh, bnk=bnk: te.matmul(
                    bnk[0:128, hh * 128:(hh + 1) * 128],
                    sq_q[:, hc * 512 + sl * 128:hc * 512 + (sl + 1) * 128],
                    kv_sb[:, (hc // 4) * 128:(hc // 4 + 1) * 128],
                    start=(hh == 0), stop=(hh == 3)),
                    inc=("mm", 1) if hh == 3 else None,
                    waits=w0 if hh == 0 else [])
            wqkv = kb.val("mm")

            def build_R(v, sl=sl, hq=hq):
                ins = None
                for hh in range(4):
                    hc = hq * 4 + hh
                    ins = v.tensor_scalar(
                        out=tqf[:, 512 + hh * 128:512 + (hh + 1) * 128],
                        in0=ones_f32[:, 0:128],
                        scalar1=r_sb[:, sl * 16 + hc:sl * 16 + hc + 1],
                        scalar2=None, op0=ALU.mult)
                return ins
            kb.op("vector", build_R, inc=("ev", 1), waits=[("ev", EV_R)])
            wrt = kb.val("ev")
            kb.op("vector", lambda v, bnk=bnk: v.tensor_tensor(
                out=q2b[:, 0:512], in0=bnk[0:128, 0:512], in1=tqf[:, 512:1024],
                op=ALU.mult), inc=("ev", 1),
                waits=[("mm", wqkv), ("mm", q2b_war_mm), ("ev", wrt)])
            wtm = kb.val("ev")
            bank_ev[2 + p] = [("ev", wtm)]
            bt = banks[6 + p]
            wtb0 = [("ev", wtm)] + bank_ev[6 + p]
            for hh in range(4):
                kb.op("tensor", lambda te, hh=hh, bt=bt: te.matmul(
                    bt[0:128, hh * 128:(hh + 1) * 128],
                    q2b[:, hh * 128:(hh + 1) * 128], ident[:],
                    start=(hh == 0), stop=(hh == 3)),
                    inc=("mm", 1) if hh == 3 else None,
                    waits=wtb0 if hh == 0 else [])
            wtb = kb.val("mm")
            q2b_war_mm = wtb
            oq = bass.AP(qkp[:].tensor, (hq * 4) * 512 + sl * 128,
                         [[16 * QT, 128], [512, 4], [1, 128]])
            iq = bass.AP(bt[:].tensor, 0, [[512, 128], [128, 4], [1, 128]])
            kb.op("vector", lambda v, oq=oq, iq=iq: v.tensor_copy(
                out=oq, in_=iq), inc=("ev", 1), waits=[("mm", wtb)])
            bank_ev[6 + p] = [("ev", kb.val("ev"))]
        EV_QKP = kb.val("ev")
        if q == TAPQ:
            tap("qkp", [128, 16 * QT], BF16); emit_tap("qkp", qkp[:])

        if stop_after == "C":
            return _finish()
        for fg in range(4):
            p = fg % 2
            dp = "dp0" if p == 0 else "dp1"
            so = p * 8192
            for gc in range(16):
                kb.op("sync", lambda s, fg=fg, gc=gc, so=so: s.dma_start(
                    out=a2b[:, so + gc * 512:so + (gc + 1) * 512],
                    in_=wo_bf[gc * 128:(gc + 1) * 128, fg * 512:(fg + 1) * 512]),
                    inc=(dp, 16),
                    waits=[("ds0", DS_TERN[0]), ("ds1", DS_TERN[1]),
                           ("mm", a2b_slot_mm[p])])
            DLS_WO = kb.val(dp)
            for i in range(SCQ):
                bp = i % 2
                bnk = banks[2 + bp]
                w0 = [(dp, DLS_WO), ("ev", EV_QKP)] + bank_ev[2 + bp]
                for gc in range(16):
                    kb.op("tensor", lambda te, i=i, gc=gc, bnk=bnk, so=so: te.matmul(
                        bnk[0:128, 0:512],
                        qkp[:, gc * 512 + i * 128:gc * 512 + (i + 1) * 128],
                        a2b[:, so + gc * 512:so + (gc + 1) * 512],
                        start=(gc == 0), stop=(gc == 15)),
                        inc=("mm", 1) if gc == 15 else None,
                        waits=w0 if gc == 0 else [])
                wo_mm = kb.val("mm")
                kb.op("vector", lambda v, i=i, fg=fg, bnk=bnk: v.tensor_tensor(
                    out=x_q[:, i * D + fg * 512:i * D + (fg + 1) * 512],
                    in0=bnk[0:128, 0:512],
                    in1=x_q[:, i * D + fg * 512:i * D + (fg + 1) * 512],
                    op=ALU.add), inc=("ev", 1), waits=[("mm", wo_mm)])
                bank_ev[2 + bp] = [("ev", kb.val("ev"))]
            a2b_slot_mm[p] = kb.val("mm")
        EV_X1 = kb.val("ev")
        if q == TAPQ:
            tap("x1", [128, SCQ * D]); emit_tap("x1", x_q[:])

        if stop_after == "E":
            return _finish()
        wq_mm_last = kb.val("mm")
        h2tm_war_mm = kb.val("mm")
        cast_ac_last = 0
        q1b_war_mm = kb.val("mm")
        lad_slot = {0: kb.val("ev"), 1: kb.val("ev")}
        q1_pend = None
        for i in range(SCQ):
            xi = x_q[:, i * D:(i + 1) * D]
            ln_stats(xi, ln2st[:, 2 * i:2 * i + 1], ln2st[:, 2 * i + 1:2 * i + 2],
                     first_waits=[("ev", EV_X1)] if i == 0 else [])
            wst = kb.val("ev")
            kb.op("scalar", lambda s_, xi=xi, i=i: s_.activation(
                out=h2tm[:], in_=xi, func=ACTF.Identity,
                scale=ln2st[:, 2 * i:2 * i + 1],
                bias=ln2st[:, 2 * i + 1:2 * i + 2]),
                inc=("ac", 1), waits=[("ev", wst), ("mm", h2tm_war_mm)])
            wh = kb.val("ac")
            d0 = transpose_pass(16, lambda t: h2tm[:, t * 128:(t + 1) * 128],
                                d4_contig(scr_bf, base=2048), bank_base=0,
                                first_waits=[("ac", wh)], evac="alt", nbanks=4)
            d1 = transpose_pass(16, lambda t: h2f[:, t * 128:(t + 1) * 128],
                                d4_inter(pa1[:], 16, D), bank_base=0, rhs=h128[:],
                                first_waits=d0, evac="alt", nbanks=4)
            d2 = transpose_pass(16, lambda t: pa1[:, t * 128:(t + 1) * 128],
                                d4_contig(pa1), bank_base=0,
                                first_waits=d1, evac="alt", nbanks=4)
            h2tm_war_mm = kb.val("mm")
            for b in range(4):
                w0 = list(d2) + bank_ev[2 + b] + (
                    [("hb", HB_DONE)] if (q == 0 and i == 0 and b == 0) else [])
                for c4 in range(4):
                    c2 = b * 4 + c4
                    kb.op("tensor", lambda te, c2=c2, c4=c4, b=b: te.matmul(
                        banks[2 + b][0:128, c4 * 128:(c4 + 1) * 128],
                        pa1[:, c2 * 128:(c2 + 1) * 128], hB1[:],
                        start=True, stop=True),
                        inc=("mm", 1) if c4 == 3 else None,
                        waits=w0 if c4 == 0 else [])
            wpb = kb.val("mm")

            def absred1(v):
                ins = None
                for b in range(4):
                    ins = v.reduce_max(out=st_[:, b:b + 1],
                                       in_=banks[2 + b][0:128, 0:512],
                                       axis=AX.X, apply_absolute_value=True)
                return ins
            kb.op("vector", absred1, inc=("ev", 1), waits=[("mm", wpb)])
            wam0 = kb.val("ev")
            kb.op("vector", lambda v: v.reduce_max(
                out=st_[:, 4:5], in_=st_[:, 0:4], axis=AX.X), inc=("ev", 1),
                waits=[("ev", wam0)])
            wam = kb.val("ev")
            kb.op("vector", lambda v, i=i: v.tensor_scalar(
                out=st_[:, 7:8], in0=st_[:, 4:5], scalar1=T1CLIP,
                scalar2=None, op0=ALU.max), inc=("ev", 1), waits=[("ev", wam)])
            wa1 = kb.val("ev")
            kb.op("vector", lambda v: v.reciprocal(
                out=st_[:, 5:6], in_=st_[:, 7:8]), inc=("ev", 1),
                waits=[("ev", wa1)])
            wrc = kb.val("ev")
            kb.op("vector", lambda v: v.tensor_scalar(
                out=st_[:, 6:7], in0=st_[:, 5:6], scalar1=127.0, scalar2=None,
                op0=ALU.mult), inc=("ev", 1),
                waits=[("ev", wrc), ("ac", cast_ac_last)])
            wscq = kb.val("ev")
            kb.op("vector", lambda v, i=i: v.reciprocal(
                out=a1c_sb[:, i:i + 1], in_=st_[:, 6:7]), inc=("ev", 1),
                waits=[("ev", wscq)])
            if q1_pend is not None:
                ip, wlad_p = q1_pend
                q1f_done = transpose_pass(
                    16, lambda t: q1b[:, t * 128:(t + 1) * 128],
                    d4_strided(arena[:], 16384 + ip * 128, 32768),
                    bank_base=6, evac="alt", nbanks=4,
                    first_waits=[("ev", wlad_p), ("mm", wq_mm_last)])
                q1b_war_mm = kb.val("mm")
            for b in range(4):
                slot = b % 2
                kb.op("scalar", lambda s_, b=b, slot=slot: s_.activation(
                    out=qi32[:, slot * 512:(slot + 1) * 512],
                    in_=banks[2 + b][0:128, 0:512], func=ACTF.Identity,
                    scale=st_[:, 6:7], bias=cst[:, 2:3]), inc=("ac", 1),
                    waits=[("ev", max(wscq, lad_slot[slot]))])
                kb.op("scalar", lambda s_, b=b, slot=slot: s_.activation(
                    out=tqf[:, slot * 512:(slot + 1) * 512],
                    in_=banks[2 + b][0:128, 0:512], func=ACTF.Identity,
                    scale=st_[:, 6:7], bias=cst[:, 3:4]), inc=("ac", 1))
                wq0 = kb.val("ac")
                bank_ev[2 + b] = [("ev", wam0), ("ac", wq0)]
                quant_tail(512, slot, q1b[:, b * 512:(b + 1) * 512], wq0,
                           extra_final=[("mm", q1b_war_mm)])
                lad_slot[slot] = kb.val("ev")
            cast_ac_last = kb.val("ac")
            q1_pend = (i, kb.val("ev"))
        ip, wlad_p = q1_pend
        q1f_done = transpose_pass(
            16, lambda t: q1b[:, t * 128:(t + 1) * 128],
            d4_strided(arena[:], 16384 + ip * 128, 32768),
            bank_base=6, evac="alt", nbanks=4,
            first_waits=[("ev", wlad_p), ("mm", wq_mm_last)])
        if q == TAPQ:
            tap("q1f", [128, 8192], BF16)
            emit_tap("q1f", arena[:, 16384:16384 + 8192])
            tap("a1c", [128, 4]); emit_tap("a1c", a1c_sb[:])

        if stop_after == "F1":
            return _finish()
        gq_war_mm = wq_mm_last
        rl_ev = {0: kb.val("ev"), 1: kb.val("ev")}
        for blk in range(8):
            p = blk % 2
            dp = "dp0" if p == 0 else "dp1"
            so = p * 8192
            for dc in range(16):
                kb.op("sync", lambda s, blk=blk, dc=dc, so=so: s.dma_start(
                    out=a2b[:, so + dc * 512:so + (dc + 1) * 512],
                    in_=wup_t[dc * 128:(dc + 1) * 128, blk * 512:(blk + 1) * 512]),
                    inc=(dp, 16),
                    waits=[("ds0", DS_TERN[0]), ("ds1", DS_TERN[1]),
                           ("mm", a2b_slot_mm[p])])
            DLS_UP = kb.val(dp)
            for fc4 in range(4):
                fc = blk * 4 + fc4
                bp = fc4 % 2
                bnk = banks[2 + bp]
                w0 = [(dp, DLS_UP)] + list(q1f_done) + bank_ev[2 + bp]
                for dc in range(16):
                    kb.op("tensor", lambda te, fc4=fc4, dc=dc, bnk=bnk, so=so: te.matmul(
                        bnk[0:128, 0:512],
                        a2b[:, so + dc * 512 + fc4 * 128:so + dc * 512 + (fc4 + 1) * 128],
                        q1f_sl(dc, 0, 512), start=(dc == 0), stop=(dc == 15)),
                        inc=("mm", 1) if dc == 15 else None,
                        waits=w0 if dc == 0 else [])
                wup_mm = kb.val("mm")
                rsc = y2f[:, bp * 512:(bp + 1) * 512]
                kb.op("vector", lambda v, bnk=bnk, rsc=rsc: v.tensor_single_scalar(
                    out=rsc, in_=bnk[0:128, 0:512], scalar=0.0,
                    op=ALU.max), inc=("ev", 1),
                    waits=[("mm", wup_mm), ("ev", rl_ev[bp])])
                wrl = kb.val("ev")
                kb.op("vector", lambda v, fc=fc, bnk=bnk, rsc=rsc: v.tensor_tensor(
                    out=gq_sl(fc, 0, 512), in0=rsc,
                    in1=bnk[0:128, 0:512], op=ALU.mult),
                    inc=("ev", 1),
                    waits=[("ev", wrl), ("mm", gq_war_mm)])
                rl_ev[bp] = kb.val("ev")
                bank_ev[2 + bp] = [("ev", kb.val("ev"))]
            a2b_slot_mm[p] = kb.val("mm")
        EV_GQ = kb.val("ev")
        if q == TAPQ:
            tap("gq", [128, 16384], BF16); emit_tap("gq", arena[:, 0:16384])

        if stop_after == "F2":
            return _finish()
        cast_ac_last = kb.val("ac")
        q2b_war_mm = kb.val("mm")
        lad_slot = {0: kb.val("ev"), 1: kb.val("ev")}
        pa2_war_mm = kb.val("mm")
        q2_pend = None
        for i in range(SCQ):
            dA = transpose_pass(
                32, lambda t, i=i: gq_sl(t, i * 128, (i + 1) * 128),
                d4_inter(pa2, 32, FF), bank_base=0, rhs=h128[:],
                first_waits=[("ev", EV_GQ), ("mm", pa2_war_mm)], evac="alt",
                nbanks=4)
            dT = transpose_pass(32, lambda t: pa2[:, t * 128:(t + 1) * 128],
                                d4_contig(scr_bf), bank_base=0,
                                first_waits=dA, evac="alt", nbanks=4)
            for piece in range(2):
                for b in range(4):
                    w0 = list(dT) + bank_ev[2 + b]
                    for c4 in range(4):
                        cc = b * 4 + c4
                        c2 = piece * 16 + cc
                        kb.op("tensor", lambda te, c2=c2, c4=c4, b=b: te.matmul(
                            banks[2 + b][0:128, c4 * 128:(c4 + 1) * 128],
                            pa2[:, c2 * 128:(c2 + 1) * 128], hB2[:],
                            start=True, stop=True),
                            inc=("mm", 1) if c4 == 3 else None,
                            waits=w0 if c4 == 0 else [])
                wpb = kb.val("mm")
                for b in range(4):
                    kb.op("vector", lambda v, piece=piece, b=b: v.reduce_max(
                        out=st_[:, piece * 4 + b:piece * 4 + b + 1],
                        in_=banks[2 + b][0:128, 0:512], axis=AX.X,
                        apply_absolute_value=True), inc=("ev", 1),
                        waits=[("mm", wpb)] if b == 0 else [])
                    wred = kb.val("ev")
                    kb.op("scalar", lambda s_, piece=piece, b=b: s_.activation(
                        out=y2f[:, piece * 2048 + b * 512:piece * 2048 + (b + 1) * 512],
                        in_=banks[2 + b][0:128, 0:512], func=ACTF.Identity),
                        inc=("ac", 1), waits=[("mm", wpb), ("ev", wred)])
                    bank_ev[2 + b] = [("ev", wred), ("ac", kb.val("ac"))]
            wy2 = kb.val("ev")
            wy2a = kb.val("ac")
            kb.op("vector", lambda v: v.reduce_max(
                out=st_[:, 8:9], in_=st_[:, 0:8], axis=AX.X), inc=("ev", 1),
                waits=[("ev", wy2)])
            w1_ = kb.val("ev")
            kb.op("vector", lambda v: v.tensor_scalar(
                out=st_[:, 9:10], in0=st_[:, 8:9], scalar1=T2CLIP, scalar2=None,
                op0=ALU.max), inc=("ev", 1), waits=[("ev", w1_)])
            wa2 = kb.val("ev")
            kb.op("vector", lambda v: v.reciprocal(
                out=st_[:, 10:11], in_=st_[:, 9:10]), inc=("ev", 1),
                waits=[("ev", wa2)])
            wrc2 = kb.val("ev")
            kb.op("vector", lambda v: v.tensor_scalar(
                out=st_[:, 11:12], in0=st_[:, 10:11], scalar1=127.0, scalar2=None,
                op0=ALU.mult), inc=("ev", 1),
                waits=[("ev", wrc2), ("ac", cast_ac_last)])
            wsc2a = kb.val("ev")
            kb.op("vector", lambda v: v.reciprocal(
                out=st_[:, 9:10], in_=st_[:, 11:12]), inc=("ev", 1),
                waits=[("ev", wsc2a)])
            wsc2 = kb.val("ev")
            kb.op("vector", lambda v, i=i: v.tensor_tensor(
                out=st_[:, 12:13], in0=a1c_sb[:, i:i + 1], in1=a1c_sb[:, i:i + 1],
                op=ALU.mult), inc=("ev", 1), waits=[("ev", wsc2)])
            wf1 = kb.val("ev")
            kb.op("vector", lambda v: v.tensor_tensor(
                out=st_[:, 13:14], in0=st_[:, 12:13], in1=st_[:, 9:10],
                op=ALU.mult), inc=("ev", 1), waits=[("ev", wf1)])
            wf2 = kb.val("ev")
            kb.op("vector", lambda v: v.tensor_tensor(
                out=st_[:, 14:15], in0=st_[:, 13:14], in1=wsc[:, 0:1],
                op=ALU.mult), inc=("ev", 1), waits=[("ev", wf2)])
            wf3 = kb.val("ev")
            kb.op("vector", lambda v: v.tensor_tensor(
                out=st_[:, 15:16], in0=st_[:, 14:15], in1=wsc[:, 0:1],
                op=ALU.mult), inc=("ev", 1), waits=[("ev", wf3)])
            wf4 = kb.val("ev")
            kb.op("vector", lambda v: v.tensor_tensor(
                out=st_[:, 12:13], in0=st_[:, 15:16], in1=wsc[:, 2:3],
                op=ALU.mult), inc=("ev", 1), waits=[("ev", wf4)])
            wf5 = kb.val("ev")
            kb.op("vector", lambda v, i=i: v.tensor_scalar(
                out=fs_sb[:, i:i + 1], in0=st_[:, 12:13],
                scalar1=7.62939453125e-06, scalar2=None,
                op0=ALU.mult), inc=("ev", 1), waits=[("ev", wf5)])
            if q2_pend is not None:
                ip, wlad_p = q2_pend
                q2f_done = transpose_pass(
                    32, lambda t: q2b[:, t * 128:(t + 1) * 128],
                    d4_strided(arena[:], ip * 128, 32768),
                    bank_base=6, evac="alt", nbanks=4,
                    first_waits=[("ev", wlad_p)])
                q2b_war_mm = kb.val("mm")
                pa2_war_mm = kb.val("mm")
            for b in range(8):
                slot = b % 2
                kb.op("scalar", lambda s_, b=b, slot=slot: s_.activation(
                    out=qi32[:, slot * 512:(slot + 1) * 512],
                    in_=y2f[:, b * 512:(b + 1) * 512], func=ACTF.Identity,
                    scale=st_[:, 11:12], bias=cst[:, 2:3]), inc=("ac", 1),
                    waits=[("ev", max(wsc2, lad_slot[slot])), ("ac", wy2a)])
                kb.op("scalar", lambda s_, b=b, slot=slot: s_.activation(
                    out=tqf[:, slot * 512:(slot + 1) * 512],
                    in_=y2f[:, b * 512:(b + 1) * 512], func=ACTF.Identity,
                    scale=st_[:, 11:12], bias=cst[:, 3:4]), inc=("ac", 1))
                wq0 = kb.val("ac")
                quant_tail(512, slot, q2b[:, b * 512:(b + 1) * 512], wq0,
                           extra_final=[("mm", q2b_war_mm)])
                lad_slot[slot] = kb.val("ev")
            cast_ac_last = kb.val("ac")
            q2_pend = (i, kb.val("ev"))
        ip, wlad_p = q2_pend
        q2f_done = transpose_pass(
            32, lambda t: q2b[:, t * 128:(t + 1) * 128],
            d4_strided(arena[:], ip * 128, 32768),
            bank_base=6, evac="alt", nbanks=4,
            first_waits=[("ev", wlad_p)])
        if stop_after == "F3":
            return _finish()
        for fg in range(4):
            for hf in range(2):
                dp = "dp0" if hf == 0 else "dp1"
                so = hf * 8192
                for d2 in range(16):
                    d2c = hf * 16 + d2
                    kb.op("sync", lambda s, fg=fg, d2=d2, d2c=d2c, so=so: s.dma_start(
                        out=a2b[:, so + d2 * 512:so + (d2 + 1) * 512],
                        in_=wdn_t[d2c * 128:(d2c + 1) * 128, fg * 512:(fg + 1) * 512]),
                        inc=(dp, 16),
                        waits=[("ds0", DS_TERN[0]), ("ds1", DS_TERN[1]),
                               ("mm", a2b_slot_mm[hf])])
            DLS_DN = (kb.val("dp0"), kb.val("dp1"))
            for hf in range(2):
                dp = "dp0" if hf == 0 else "dp1"
                so = hf * 8192
                for i in range(SCQ):
                    bnk = banks[2 + i]
                    w0 = [(dp, DLS_DN[hf])] + list(q2f_done) + (
                        bank_ev[2 + i] if hf == 0 else [])
                    for d2 in range(16):
                        d2c = hf * 16 + d2
                        kb.op("tensor", lambda te, i=i, d2=d2, d2c=d2c, bnk=bnk, so=so, hf=hf: te.matmul(
                            bnk[0:128, 0:512],
                            arena[:, d2c * 512 + i * 128:d2c * 512 + (i + 1) * 128],
                            a2b[:, so + d2 * 512:so + (d2 + 1) * 512],
                            start=(hf == 0 and d2 == 0), stop=(hf == 1 and d2 == 15)),
                            inc=("mm", 1) if d2 == 15 else None,
                            waits=w0 if d2 == 0 else [])
                a2b_slot_mm[hf] = kb.val("mm")
            wdn_mm = kb.val("mm")
            for i in range(SCQ):
                kb.op("vector", lambda v, i=i, fg=fg: v.scalar_tensor_tensor(
                    out=x_q[:, i * D + fg * 512:i * D + (fg + 1) * 512],
                    in0=banks[2 + i][0:128, 0:512], scalar=fs_sb[:, i:i + 1],
                    in1=x_q[:, i * D + fg * 512:i * D + (fg + 1) * 512],
                    op0=ALU.mult, op1=ALU.add), inc=("ev", 1),
                    waits=[("mm", wdn_mm)])
                bank_ev[2 + i] = [("ev", kb.val("ev"))]
        EV_Y = kb.val("ev")
        arena_war_mm = kb.val("mm")
        for i in range(SCQ):
            kb.op("sync", lambda s, i=i, sc0=sc0: s.dma_start(
                out=y_out[(sc0 + i) * 128:(sc0 + i + 1) * 128, :],
                in_=x_q[:, i * D:(i + 1) * D]), inc=("dst", 16),
                waits=[("ev", EV_Y)] if i == 0 else [])

    DST_ALL = kb.val("dst")
    kb.op("sync", lambda s: s.wait_ge(kb.sems["dst"], DST_ALL))

    return _finish()


def host_inputs(L0_attn_Wq, L0_attn_bq, L0_attn_Wk, L0_attn_bk, L0_attn_Wv,
                L0_attn_bv, L0_attn_Wo, L0_ffn_w_up, L0_ffn_w_down):
    asc = np.ascontiguousarray
    return {
        "wqT": asc(np.asarray(L0_attn_Wq, np.float32).T),
        "wkT": asc(np.asarray(L0_attn_Wk, np.float32).T),
        "wvT": asc(np.asarray(L0_attn_Wv, np.float32).T),
        "woT": asc(np.asarray(L0_attn_Wo, np.float32).T),
        "wupT": asc(np.asarray(L0_ffn_w_up, np.float32).T[PERM_F1, :]),
        "wdnT": asc(np.asarray(L0_ffn_w_down, np.float32).T[PERM_F2, :]),
        "bq_in": asc(np.asarray(L0_attn_bq, np.float32).reshape(16, 128).T),
        "bk_in": asc(np.asarray(L0_attn_bk, np.float32).reshape(1, 512)),
        "bv_in": asc(np.asarray(L0_attn_bv, np.float32).reshape(1, 512)),
    }


def kernel(L0_attn_hidden_states, L0_attn_Wq, L0_attn_bq, L0_attn_Wk, L0_attn_bk,
           L0_attn_Wv, L0_attn_bv, L0_attn_Wo, L0_ffn_w_up, L0_ffn_w_down):
    asc = np.ascontiguousarray
    x = np.asarray(L0_attn_hidden_states, np.float32)
    common = host_inputs(L0_attn_Wq, L0_attn_bq, L0_attn_Wk, L0_attn_bk,
                         L0_attn_Wv, L0_attn_bv, L0_attn_Wo, L0_ffn_w_up,
                         L0_ffn_w_down)
    nc, _ = build()
    in_maps = [dict(common, x_in=asc(x[c])) for c in range(8)]
    res = run_bass_kernel_spmd(nc, in_maps, core_ids=list(range(8)))
    return np.stack([res.results[c]["y_out"] for c in range(8)], axis=0)


# version 20
# speedup vs baseline: 1.0194x; 1.0194x over previous
import contextlib
import math

import numpy as np

import concourse.bass as bass
import concourse.mybir as mybir
from concourse.bass_utils import run_bass_kernel_spmd

F32 = mybir.dt.float32
BF16 = mybir.dt.bfloat16
I32 = mybir.dt.int32
AX = mybir.AxisListType
ALU = mybir.AluOpType
ACTF = mybir.ActivationFunctionType

S = 2048
D = 2048
FF = 4096
NSC = 16
NQ = 4
SCQ = 4
QT = 512

EPS_LN = 1e-5
EPS_ATTN = 1e-6
T1CLIP = float(np.float32(1e-5 * math.sqrt(2048.0)))
T2CLIP = float(np.float32(1e-5 * 64.0))

_s1 = np.arange(2048)
PERM_F1 = ((_s1 & 15) << 7) | ((_s1 >> 7) << 3) | ((_s1 >> 4) & 7)
_s2 = np.arange(4096)
PERM_F2 = ((_s2 & 31) << 7) | ((_s2 >> 7) << 2) | ((_s2 >> 5) & 3)

ENGINES = ("tensor", "vector", "scalar", "gpsimd", "sync")
SEMS = ("mm", "ev", "ac", "gp", "hb", "dp0", "dp1", "ds0", "ds1", "dgl",
        "dst", "dbk", "dwkv", "dg0", "dg1", "dbq")
COMPUTE_SEMS = ("mm", "ev", "ac", "gp")


class KB:
    def __init__(self, nc):
        self.nc = nc
        self.steps = {e: [] for e in ENGINES}
        self.sems = {s: nc.ctx.enter_context(nc.semaphore(s)) for s in SEMS}
        self.counts = {s: 0 for s in SEMS}
        self._stage_pending = {}

    def val(self, s):
        return self.counts[s]

    def snapshot(self):
        return {s: self.counts[s] for s in COMPUTE_SEMS}

    def stage(self, engines=ENGINES):
        snap = self.snapshot()
        for e in engines:
            self._stage_pending[e] = snap

    def op(self, eng, emit, inc=None, waits=()):
        for w, v in waits:
            assert v <= self.counts[w], f"wait {w}={v} > {self.counts[w]}"
        pend = self._stage_pending.pop(eng, None)
        if inc is not None:
            self.counts[inc[0]] += inc[1]
        sems = self.sems

        def thunk(eobj, _emit=emit, _waits=list(waits), _inc=inc, _pend=pend):
            if _pend:
                for s, v in _pend.items():
                    if v > 0:
                        eobj.wait_ge(sems[s], v)
            for w, v in _waits:
                if v > 0:
                    eobj.wait_ge(sems[w], v)
            ins = _emit(eobj)
            if _inc is not None:
                assert ins is not None
                ins.then_inc(sems[_inc[0]], _inc[1])
            return ins
        self.steps[eng].append(thunk)


def build(debug_taps=(), stop_after=None, TAPQ=0):
    nc = bass.Bass("TRN2", target_bir_lowering=False, debug=False)
    kb = KB(nc)
    taps = set(debug_taps)

    x_in = nc.dram_tensor("x_in", [S, D], F32, kind="ExternalInput").ap()
    wqT = nc.dram_tensor("wqT", [D, D], F32, kind="ExternalInput").ap()
    wkT_d = nc.dram_tensor("wkT", [D, 512], F32, kind="ExternalInput").ap()
    wvT_d = nc.dram_tensor("wvT", [D, 512], F32, kind="ExternalInput").ap()
    woT = nc.dram_tensor("woT", [D, D], F32, kind="ExternalInput").ap()
    wupT = nc.dram_tensor("wupT", [D, FF], F32, kind="ExternalInput").ap()
    wdnT = nc.dram_tensor("wdnT", [FF, D], F32, kind="ExternalInput").ap()
    bq_in = nc.dram_tensor("bq_in", [128, 16], F32, kind="ExternalInput").ap()
    bk_in = nc.dram_tensor("bk_in", [1, 512], F32, kind="ExternalInput").ap()
    bv_in = nc.dram_tensor("bv_in", [1, 512], F32, kind="ExternalInput").ap()
    y_out = nc.dram_tensor("y_out", [S, D], F32, kind="ExternalOutput").ap()
    kind_s = "ExternalOutput" if "prep" in taps else "Internal"
    wq_bf = nc.dram_tensor("wq_bf", [D, D], BF16, kind=kind_s).ap()
    wo_bf = nc.dram_tensor("wo_bf", [D, D], BF16, kind=kind_s).ap()
    wup_t = nc.dram_tensor("wup_t", [D, FF], BF16, kind=kind_s).ap()
    wdn_t = nc.dram_tensor("wdn_t", [FF, D], BF16, kind=kind_s).ap()

    tap_t = {}

    def tap(name, shape, dtype=F32):
        if name in taps:
            if name not in tap_t:
                tap_t[name] = nc.dram_tensor("tap_" + name, shape, dtype,
                                             kind="ExternalOutput").ap()
            return tap_t[name]
        return None

    def emit_tap(name, src_ap):
        t = tap_t.get(name)
        if t is not None:
            kb.stage(("sync",))
            kb.op("sync", lambda s, t=t, src_ap=src_ap: s.dma_start(
                out=t, in_=src_ap), inc=("dst", 16))
            wt_ = kb.val("dst")
            kb.op("vector", lambda v: v.engine_nop(), inc=("ev", 1),
                  waits=[("dst", wt_)])

    es = nc.ctx

    def sb(name, shape, dtype):
        return es.enter_context(nc.sbuf_tensor(name, shape, dtype))

    cst = sb("cst", [128, 8], F32)
    ones_bf = sb("ones_bf", [128, 128], BF16)
    ones_f32 = sb("ones_f32", [128, 128], F32)
    ident = sb("ident", [128, 128], BF16)
    h128 = sb("h128", [128, 128], BF16)
    hB1 = sb("hB1", [128, 128], BF16)
    hB2 = sb("hB2", [128, 128], BF16)
    negscr = sb("negscr", [128, 128], BF16)
    bq_sb = sb("bq_sb", [128, 16], F32)
    stats = sb("stats", [128, 32], F32)
    kv_sb = sb("kv_sb", [128, 512], BF16)
    ks_sb = sb("ks_sb", [128, 4], BF16)
    wsum = sb("wsum", [128, 2], F32)
    wsc = sb("wsc", [128, 4], F32)
    r_sb = sb("r_sb", [128, 64], F32)
    a1c_sb = sb("a1c_sb", [128, 4], F32)
    fs_sb = sb("fs_sb", [128, 4], F32)
    st_ = sb("sstat", [128, 16], F32)
    ln2st = sb("ln2st", [128, 8], F32)

    x_q = sb("x_q", [128, SCQ * D], F32)
    arena = sb("arena", [128, 32768], BF16)
    a2b = sb("a2b", [128, 16384], BF16)
    h1f = sb("h1f", [128, 16 * QT], BF16)
    sq_q = sb("sq_q", [128, 16 * QT], BF16)
    scr_bf = sb("scr_bf", [128, FF], BF16)
    h2tm = scr_bf[:, 0:2048]
    h2f = scr_bf[:, 2048:4096]
    pa2 = scr_bf[:, :]
    pa1 = sb("pa1", [128, D], BF16)
    y2f = sb("y2f", [128, FF], F32)
    xsc = y2f[:, 2048:4096]
    q2b = sb("q2b", [128, FF], BF16)
    qi32 = sb("qi32", [128, 1024], I32)
    tqf = sb("tqf", [128, 1024], F32)

    a2f = x_q
    corr = h1f[:, 2048:3072]
    cdq = h1f[:, 3072:4096]
    ltqs = h1f[:, 0:1024]
    lqbf = h1f[:, 1024:2048]
    bk_sb = arena[0:1, 16384:16896]
    bv_sb = arena[0:1, 16896:17408]
    qkp = h1f
    q1b = q2b
    skv = sq_q
    banks = [es.enter_context(nc.psum_tensor(f"pb{i}", [128, 512], F32))
             for i in range(8)]

    bank_ev = {i: [] for i in range(8)}

    def elu_tiles_b(p):
        base = p * 2048
        return (h1f[:, base:base + 512], h1f[:, base + 512:base + 1024],
                h1f[:, base + 1024:base + 1536])

    def elu_tiles(p):
        if p == 0:
            return tqf[:, 0:512], tqf[:, 512:1024], y2f[:, 0:512]
        return y2f[:, 1024:1536], y2f[:, 1536:2048], y2f[:, 512:1024]

    def wk_tile(dc):
        return arena[:, dc * 1024:dc * 1024 + 512]

    def wv_tile(dc):
        return arena[:, dc * 1024 + 512:(dc + 1) * 1024]

    def wq_tile(dc):
        return arena[:, dc * 2048:(dc + 1) * 2048]

    def gq_sl(fc, lo, hi):
        return arena[:, fc * 512 + lo:fc * 512 + hi]

    def q1f_sl(dc, lo, hi):
        return arena[:, 16384 + dc * 512 + lo:16384 + dc * 512 + hi]

    def g_seeds1(g):
        for col, v in enumerate((EPS_LN, -0.5, 128.0, 127.5, T1CLIP, T2CLIP,
                                 1.0, 128.5)):
            g.memset(cst[:, col:col + 1], v)
        g.memset(ones_bf[:], 1.0)
        g.memset(ones_f32[:], 1.0)
        g.memset(ident[:], 1.0)
        for h in (h128, hB1, hB2):
            g.memset(h[:], 0.0)
        g.memset(wsum[:], 0.0)
        return g.memset(negscr[:], 0.0)
    kb.op("gpsimd", g_seeds1, inc=("gp", 1))

    def g_seeds2(g):
        for h in (h128, hB1, hB2):
            g.memset(h[0:1, 0:1], 1.0)
        return g.affine_select(ident[:], ident[:], pattern=[[-1, 128]],
                               compare_op=ALU.is_equal, fill=0.0,
                               channel_multiplier=1)
    kb.op("gpsimd", g_seeds2, inc=("gp", 1), waits=[("gp", 1)])
    kb.op("sync", lambda s: s.dma_start(out=bq_sb[:], in_=bq_in), inc=("dbq", 16))
    kb.op("gpsimd", lambda g: g.dma_start(out=bk_sb[:], in_=bk_in), inc=("dbk", 16))
    kb.op("gpsimd", lambda g: g.dma_start(out=bv_sb[:], in_=bv_in), inc=("dbk", 16))

    kb.stage(("vector", "scalar"))
    jobs = []
    k = 1
    while k < 128:
        jobs.append(("dbl", h128, k)); k *= 2
    k = 1
    while k < 16:
        jobs.append(("dbl", hB1, k)); k *= 2
    k = 16
    while k < 128:
        jobs.append(("rep", hB1, k)); k *= 2
    k = 1
    while k < 32:
        jobs.append(("dbl", hB2, k)); k *= 2
    k = 32
    while k < 128:
        jobs.append(("rep", hB2, k)); k *= 2
    for kind, ht, k in jobs:
        if kind == "dbl":
            kb.op("vector",
                  lambda v, ht=ht, k=k: v.tensor_copy(out=ht[0:k, k:2 * k],
                                                      in_=ht[0:k, 0:k]),
                  waits=[("gp", 2), ("hb", kb.val("hb"))])
            kb.op("vector",
                  lambda v, ht=ht, k=k: v.tensor_scalar(
                      out=negscr[0:k, 0:k], in0=ht[0:k, 0:k], scalar1=-1.0,
                      scalar2=None, op0=ALU.mult), inc=("hb", 1))
            w = kb.val("hb")
            kb.op("scalar", lambda sc, ht=ht, k=k: sc.dma_start(
                out=ht[k:2 * k, 0:k], in_=ht[0:k, 0:k]),
                inc=("hb", 16), waits=[("hb", w)])
            kb.op("scalar", lambda sc, ht=ht, k=k: sc.dma_start(
                out=ht[k:2 * k, k:2 * k], in_=negscr[0:k, 0:k]), inc=("hb", 16))
        else:
            w = kb.val("hb")
            kb.op("scalar", lambda sc, ht=ht, k=k: sc.dma_start(
                out=ht[k:2 * k, k:2 * k], in_=ht[0:k, 0:k]),
                inc=("hb", 16), waits=[("hb", w)])
    HB_DONE = kb.val("hb")

    def quant_tail(width, slot, out_ap, wq0, extra_final=()):
        n = width
        o = slot * n
        qs = qi32[:, o:o + n]
        ts = tqf[:, o:o + n]
        l_corr = corr[:, o:o + n]
        l_cd = cdq[:, o:o + n]
        l_up = ltqs[:, o:o + n]
        l_dn = lqbf[:, o:o + n]
        kb.op("vector", lambda v: v.tensor_tensor(
            out=l_corr, in0=ts, in1=qs,
            op=ALU.is_ge), inc=("ev", 1), waits=[("ac", wq0)])
        kb.op("vector", lambda v: v.scalar_tensor_tensor(
            out=l_cd, in0=qs, scalar=-1.0,
            in1=ts, op0=ALU.add, op1=ALU.is_gt), inc=("ev", 1))
        kb.op("vector", lambda v: v.scalar_tensor_tensor(
            out=l_up, in0=qs, scalar=1.0,
            in1=ts, op0=ALU.add, op1=ALU.is_le), inc=("ev", 1))
        kb.op("vector", lambda v: v.scalar_tensor_tensor(
            out=l_dn, in0=qs, scalar=-2.0,
            in1=ts, op0=ALU.add, op1=ALU.is_gt), inc=("ev", 1))
        w4 = kb.val("ev")
        kb.op("vector", lambda v: v.tensor_tensor(
            out=l_corr, in0=l_corr, in1=l_cd,
            op=ALU.subtract), inc=("ev", 1), waits=[("ev", w4)])
        kb.op("vector", lambda v: v.tensor_tensor(
            out=l_up, in0=l_up, in1=l_dn,
            op=ALU.subtract), inc=("ev", 1))
        w6 = kb.val("ev")
        kb.op("vector", lambda v: v.tensor_tensor(
            out=l_corr, in0=l_corr, in1=l_up,
            op=ALU.add), inc=("ev", 1), waits=[("ev", w6)])
        wcn = kb.val("ev")
        kb.op("vector", lambda v, out_ap=out_ap: v.scalar_tensor_tensor(
            out=out_ap, in0=qs, scalar=-128.0,
            in1=l_corr, op0=ALU.add, op1=ALU.add),
            inc=("ev", 1), waits=[("ev", wcn)] + list(extra_final))

    def ln_stats(x_ap, out0, out1, first_waits=(), scratch=None):
        scr = y2f[:, 0:2048] if scratch is None else scratch
        kb.op("scalar", lambda s_, scr=scr: s_.activation(
            out=scr, in_=x_ap, func=ACTF.Identity,
            accum_out=st_[:, 0:1]), inc=("ac", 1), waits=list(first_waits))
        w0 = kb.val("ac")
        kb.op("scalar", lambda s_, scr=scr: s_.activation(
            out=scr, in_=x_ap, func=ACTF.Square,
            accum_out=st_[:, 1:2]), inc=("ac", 1), waits=[("ac", w0)])
        w1 = kb.val("ac")

        def chain1(v):
            v.tensor_scalar(out=st_[:, 2:3], in0=st_[:, 0:1],
                            scalar1=1.0 / 2048, scalar2=None, op0=ALU.mult)
            return v.tensor_scalar(out=st_[:, 3:4], in0=st_[:, 1:2],
                                   scalar1=1.0 / 2048, scalar2=None, op0=ALU.mult)
        kb.op("vector", chain1, inc=("ev", 1), waits=[("ac", w1)])
        w2 = kb.val("ev")
        kb.op("vector", lambda v: v.tensor_tensor(
            out=st_[:, 4:5], in0=st_[:, 2:3], in1=st_[:, 2:3], op=ALU.mult),
            inc=("ev", 1), waits=[("ev", w2)])
        w3 = kb.val("ev")
        kb.op("vector", lambda v: v.tensor_tensor(
            out=st_[:, 5:6], in0=st_[:, 3:4], in1=st_[:, 4:5], op=ALU.subtract),
            inc=("ev", 1), waits=[("ev", w3)])
        w4 = kb.val("ev")
        kb.op("scalar", lambda s_: s_.activation(
            out=st_[:, 6:7], in_=st_[:, 5:6], func=ACTF.Ln, bias=cst[:, 0:1],
            scale=1.0), inc=("ac", 1), waits=[("ev", w4)])
        w5 = kb.val("ac")
        kb.op("scalar", lambda s_: s_.activation(
            out=out0, in_=st_[:, 6:7], func=ACTF.Exp, bias=0.0,
            scale=cst[:, 1:2]), inc=("ac", 1), waits=[("ac", w5)])
        w6 = kb.val("ac")
        kb.op("vector", lambda v: v.scalar_tensor_tensor(
            out=out1, in0=st_[:, 2:3], scalar=-1.0, in1=out0,
            op0=ALU.mult, op1=ALU.mult), inc=("ev", 1), waits=[("ac", w6)])

    def transpose_pass(n, src_sl, dst4, bank_base=0, rhs=None, first_waits=(),
                       evac="vector", nbanks=2):
        rhs = ident[:] if rhs is None else rhs
        assert n % 4 == 0
        last = {}
        blist = ([bank_base, bank_base + 1] if nbanks == 2 else [0, 1, 6, 7])
        for g in range(n // 4):
            bi = blist[g % len(blist)]
            bk_ = banks[bi]
            w0 = list(bank_ev[bi]) + (list(first_waits) if g == 0 else [])
            for t4 in range(4):
                t = g * 4 + t4
                kb.op("tensor", lambda te, t=t, t4=t4, bk_=bk_, rhs=rhs: te.matmul(
                    bk_[0:128, t4 * 128:(t4 + 1) * 128], src_sl(t), rhs,
                    start=True, stop=True),
                    inc=("mm", 1) if t4 == 3 else None,
                    waits=w0 if t4 == 0 else [])
            wm = kb.val("mm")
            out_ap, in_ap = dst4(g, bk_)
            eng = evac if evac != "alt" else ("vector" if g % 2 == 0 else "scalar")
            if eng == "vector":
                kb.op("vector", lambda v, out_ap=out_ap, in_ap=in_ap: v.tensor_copy(
                    out=out_ap, in_=in_ap), inc=("ev", 1), waits=[("mm", wm)])
                bank_ev[bi] = [("ev", kb.val("ev"))]
                last["ev"] = kb.val("ev")
            else:
                kb.op("scalar", lambda s_, out_ap=out_ap, in_ap=in_ap: s_.activation(
                    out=out_ap, in_=in_ap, func=ACTF.Identity),
                    inc=("ac", 1), waits=[("mm", wm)])
                bank_ev[bi] = [("ac", kb.val("ac"))]
                last["ac"] = kb.val("ac")
        return [(s, v) for s, v in last.items()]

    def d4_contig(buf, base=0):
        return lambda g, bk: (buf[:, base + g * 512:base + (g + 1) * 512],
                              bk[0:128, 0:512])

    def d4_strided(ten_ap, base, ppitch):
        def f(g, bk):
            return (bass.AP(ten_ap.tensor, base + g * 4 * 512,
                            [[ppitch, 128], [512, 4], [1, 128]]),
                    bass.AP(bk[:].tensor, 0, [[512, 128], [128, 4], [1, 128]]))
        return f

    def d4_inter(ten_ap, nch, ppitch):
        def f(g, bk):
            return (bass.AP(ten_ap.tensor, g * 4,
                            [[ppitch, 128], [nch, 128], [1, 4]]),
                    bass.AP(bk[:].tensor, 0, [[512, 128], [1, 128], [128, 4]]))
        return f

    def _finish():
        with nc.Block() as block:
            block.tensor(lambda e: [f(e) for f in kb.steps["tensor"]])
            block.vector(lambda e: [f(e) for f in kb.steps["vector"]])
            block.scalar(lambda e: [f(e) for f in kb.steps["scalar"]])
            block.gpsimd(lambda e: [f(e) for f in kb.steps["gpsimd"]])
            block.sync(lambda e: [f(e) for f in kb.steps["sync"]])
        return nc, sorted(tap_t)

    bst = {"war0": [], "war1": [], "h2tm_war_mm": 0, "exp_ac": 0,
           "skv_war": 0, "next": 0}

    def emit_b_chunk():
        sc = bst["next"]
        if sc >= 16:
            return
        bst["next"] = sc + 1
        sl = sc % 2
        xs = y2f[:, 2048:4096] if sl == 0 else y2f[:, 0:2048]
        kb.op("sync", lambda s, sc=sc, xs=xs: s.dma_start(
            out=xs, in_=x_in[sc * 128:(sc + 1) * 128, :]), inc=("dgl", 16),
            waits=bst["war0" if sl == 0 else "war1"])
        wl = kb.val("dgl")
        ln_stats(xs, stats[:, 2 * sc:2 * sc + 1],
                 stats[:, 2 * sc + 1:2 * sc + 2],
                 first_waits=[("dgl", wl)], scratch=a2b[:, 12288:14336])
        wst = kb.val("ev")
        kb.op("scalar", lambda s_, sc=sc, xs=xs: s_.activation(
            out=h2tm[:], in_=xs, func=ACTF.Identity,
            scale=stats[:, 2 * sc:2 * sc + 1],
            bias=stats[:, 2 * sc + 1:2 * sc + 2]),
            inc=("ac", 1), waits=[("ev", wst), ("mm", bst["h2tm_war_mm"])])
        wh1 = kb.val("ac")
        bst["war0" if sl == 0 else "war1"] = [("ac", wh1)]
        fw = [("ac", wh1)]
        if sc == 0:
            fw += [("hb", HB_DONE), ("gp", 2)]
        transpose_pass(16, lambda t: h2tm[:, t * 128:(t + 1) * 128],
                       d4_contig(scr_bf, base=2048),
                       bank_base=0, first_waits=fw, nbanks=4)
        bst["h2tm_war_mm"] = kb.val("mm")
        wh1f = kb.val("ev")
        for which, bi, wt, brow in (("k", 2, wk_tile, bk_sb), ("v", 3, wv_tile, bv_sb)):
            w0 = [("ev", wh1f), ("dwkv", DWKV)] + bank_ev[bi]
            for dc in range(16):
                kb.op("tensor", lambda te, dc=dc, bi=bi, wt=wt: te.matmul(
                    banks[bi][0:128, 0:512], h2f[:, dc * 128:(dc + 1) * 128],
                    wt(dc), start=(dc == 0), stop=False),
                    waits=w0 if dc == 0 else [])
            kb.op("tensor", lambda te, bi=bi, brow=brow: te.matmul(
                banks[bi][0:128, 0:512], ones_bf[0:1, 0:128], brow[0:1, :],
                start=False, stop=True), inc=("mm", 1), waits=[("dbk", 32)])
        wkvm = kb.val("mm")
        eu, ee, er = elu_tiles_b(sc % 2)
        kb.op("vector", lambda v, eu=eu: v.tensor_single_scalar(
            out=eu, in_=banks[2][0:128, 0:512], scalar=0.0, op=ALU.min),
            inc=("ev", 1), waits=[("mm", wkvm), ("ac", bst["exp_ac"])])
        wu = kb.val("ev")
        kb.op("scalar", lambda s_, er=er: s_.activation(
            out=er, in_=banks[2][0:128, 0:512], func=ACTF.Relu),
            inc=("ac", 1), waits=[("mm", wkvm), ("ev", wu)])
        wrel = kb.val("ac")
        kb.op("scalar", lambda s_, eu=eu, ee=ee: s_.activation(
            out=ee, in_=eu, func=ACTF.Exp), inc=("ac", 1),
            waits=[("ev", wu)])
        wee = kb.val("ac")
        bst["exp_ac"] = wee
        bank_ev[2] = [("ev", wu), ("ac", wrel)]
        kb.op("vector", lambda v, ee=ee, er=er: v.tensor_tensor(
            out=skv[:, 0:512], in0=ee, in1=er, op=ALU.add),
            inc=("ev", 1), waits=[("ac", wee), ("mm", bst["skv_war"])])
        kb.op("vector", lambda v: v.tensor_copy(
            out=skv[:, 512:1024], in_=banks[3][0:128, 0:512]), inc=("ev", 1),
            waits=[("mm", wkvm), ("mm", bst["skv_war"])])
        wsk = kb.val("ev")
        bank_ev[3] = [("ev", wsk)]
        for kh in range(4):
            kb.op("tensor", lambda te, kh=kh, sc=sc: te.matmul(
                banks[4][0:128, kh * 128:(kh + 1) * 128],
                skv[:, kh * 128:(kh + 1) * 128],
                skv[:, 512 + kh * 128:512 + (kh + 1) * 128],
                start=(sc == 0 and kh == 0), stop=(sc == NSC - 1 and kh == 3)),
                inc=("mm", 1) if (sc == NSC - 1 and kh == 3) else None,
                waits=[("ev", wsk)] if kh == 0 else [])
        for kh in range(4):
            kb.op("tensor", lambda te, kh=kh, sc=sc: te.matmul(
                banks[5][0:128, kh:kh + 1],
                skv[:, kh * 128:(kh + 1) * 128], ones_bf[:, 0:1],
                start=(sc == 0 and kh == 0), stop=(sc == NSC - 1 and kh == 3)),
                inc=("mm", 1))
        bst["skv_war"] = kb.val("mm")


    for dc in range(16):
        kb.op("gpsimd", lambda g, dc=dc: g.dma_start(
            out=wk_tile(dc), in_=wkT_d[dc * 128:(dc + 1) * 128, :]), inc=("dwkv", 16))
        kb.op("gpsimd", lambda g, dc=dc: g.dma_start(
            out=wv_tile(dc), in_=wvT_d[dc * 128:(dc + 1) * 128, :]), inc=("dwkv", 16))
    DWKV = kb.val("dwkv")

    for src, dst in ((wqT, wq_bf), (woT, wo_bf)):
        for dc in range(16):
            par = dc % 2
            dp, ds = ("dg0", "ds0") if par == 0 else ("dg1", "ds1")
            ap = a2b[:, par * 2048:(par + 1) * 2048]
            wstore = kb.val(ds)
            kb.op("gpsimd", lambda g, ap=ap, src=src, dc=dc: g.dma_start(
                out=ap, in_=src[dc * 128:(dc + 1) * 128, :]),
                inc=(dp, 16), waits=[(ds, wstore)])
            wload = kb.val(dp)
            kb.op("scalar", lambda sc, ap=ap, dst=dst, dc=dc: sc.dma_start(
                out=dst[dc * 128:(dc + 1) * 128, :], in_=ap),
                inc=(ds, 16), waits=[(dp, wload)])
    DS_WQO = (kb.val("ds0"), kb.val("ds1"))

    sum_ac = {0: 0, 1: 0}
    sum_ev = 0
    for wi, (src_, rows, ncols) in enumerate(((wupT, D, FF), (wdnT, FF, D))):
        for t in range(rows // 128):
            par = t % 2
            dp = "dp0" if par == 0 else "dp1"
            ap = a2f[:, par * 4096:par * 4096 + ncols]
            kb.op("sync", lambda s, ap=ap, src_=src_, t=t: s.dma_start(
                out=ap, in_=src_[t * 128:(t + 1) * 128, :]),
                inc=(dp, 16), waits=[("ac", sum_ac[par])])
            wl = kb.val(dp)
            kb.op("scalar", lambda sc, ap=ap, ncols=ncols: sc.activation(
                out=arena[:, 32768 - ncols:32768], in_=ap, func=ACTF.Abs,
                accum_out=st_[:, 8:9]), inc=("ac", 1),
                waits=[(dp, wl), ("ev", sum_ev)])
            wa = kb.val("ac")
            sum_ac[par] = wa
            kb.op("vector", lambda v, wi=wi: v.tensor_tensor(
                out=wsum[:, wi:wi + 1], in0=wsum[:, wi:wi + 1],
                in1=st_[:, 8:9], op=ALU.add), inc=("ev", 1),
                waits=[("ac", wa), ("ev", sum_ev), ("gp", 2)])
            sum_ev = kb.val("ev")
            if (wi * 16 + t) % 5 == 4:
                emit_b_chunk()
    kb.op("tensor", lambda te: te.matmul(
        banks[7][0:128, 0:1], ones_f32[:], wsum[:, 0:1], start=True, stop=False),
        waits=[("ev", kb.val("ev")), ("gp", 2)])
    kb.op("tensor", lambda te: te.matmul(
        banks[7][0:128, 1:2], ones_f32[:], wsum[:, 1:2], start=False, stop=True),
        inc=("mm", 1))
    MM_WSUM = kb.val("mm")
    kb.op("vector", lambda v: v.tensor_scalar(
        out=wsc[:, 0:1], in0=banks[7][0:128, 0:1], scalar1=1.0 / (D * FF),
        scalar2=EPS_LN, op0=ALU.mult, op1=ALU.max), inc=("ev", 1),
        waits=[("mm", MM_WSUM)])
    kb.op("vector", lambda v: v.tensor_scalar(
        out=wsc[:, 2:3], in0=banks[7][0:128, 1:2], scalar1=1.0 / (D * FF),
        scalar2=EPS_LN, op0=ALU.mult, op1=ALU.max), inc=("ev", 1))
    w = kb.val("ev")
    bank_ev[7] = [("ev", w)]
    kb.op("vector", lambda v: v.reciprocal(out=wsc[:, 1:2], in_=wsc[:, 0:1]),
          inc=("ev", 1), waits=[("ev", w)])
    kb.op("vector", lambda v: v.reciprocal(out=wsc[:, 3:4], in_=wsc[:, 2:3]),
          inc=("ev", 1))
    EV_WSC = kb.val("ev")
    tap("wsc", [128, 4]); emit_tap("wsc", wsc[:])

    DSB = (kb.val("ds0"), kb.val("ds1"))
    t_lad = {0: kb.val("ev"), 1: kb.val("ev")}
    fap_ac = {0: kb.val("ac"), 1: kb.val("ac")}
    pc = [0]
    for wi, (src_, dst, rows, cols, invcol) in enumerate(
            ((wupT, wup_t, D, FF, 1), (wdnT, wdn_t, FF, D, 3))):
        for t in range(rows // 128):
            par = t % 2
            dp, ds = ("dp0", "ds0") if par == 0 else ("dp1", "ds1")
            fap = a2f[:, par * 4096:par * 4096 + cols]
            bap = a2b[:, par * 8192:par * 8192 + cols]
            wprev_b = kb.val(ds)
            kb.op("sync", lambda s, fap=fap, src_=src_, t=t: s.dma_start(
                out=fap, in_=src_[t * 128:(t + 1) * 128, :]),
                inc=(dp, 16), waits=[("ac", fap_ac[par])])
            wl = kb.val(dp)
            for sub in range(cols // 512):
                slot = pc[0] % 2
                pc[0] += 1
                ts_ = tqf[:, slot * 512:(slot + 1) * 512]
                cr = pa1[:, slot * 512:slot * 512 + 512]
                cd_ = pa1[:, 1024 + slot * 512:1024 + slot * 512 + 512]
                fsub = a2f[:, par * 4096 + sub * 512:par * 4096 + (sub + 1) * 512]
                bsub = a2b[:, par * 8192 + sub * 512:par * 8192 + (sub + 1) * 512]
                kb.op("scalar", lambda sc, fsub=fsub, invcol=invcol, ts_=ts_: sc.activation(
                    out=ts_, in_=fsub, func=ACTF.Identity,
                    scale=wsc[:, invcol:invcol + 1]),
                    inc=("ac", 1),
                    waits=[(dp, wl), ("ev", max(EV_WSC, t_lad[slot])),
                           ("gp", kb.val("gp"))])
                wa = kb.val("ac")
                kb.op("vector", lambda v, cr=cr, ts_=ts_: v.tensor_single_scalar(
                    out=cr, in_=ts_, scalar=0.5, op=ALU.is_ge),
                    inc=("ev", 1), waits=[("ac", wa)])
                kb.op("vector", lambda v, cd_=cd_, ts_=ts_: v.tensor_single_scalar(
                    out=cd_, in_=ts_, scalar=-0.5, op=ALU.is_le),
                    inc=("ev", 1))
                wcd = kb.val("ev")
                kb.op("vector", lambda v, bsub=bsub, cr=cr, cd_=cd_: v.tensor_tensor(
                    out=bsub, in0=cr, in1=cd_, op=ALU.subtract), inc=("ev", 1),
                    waits=[("ev", wcd), (ds, wprev_b),
                           ("ds0", DSB[0]), ("ds1", DSB[1])])
                t_lad[slot] = kb.val("ev")
                fap_ac[par] = kb.val("ac")
            we = kb.val("ev")
            kb.op("scalar", lambda sc, bap=bap, dst=dst, t=t: sc.dma_start(
                out=dst[t * 128:(t + 1) * 128, :], in_=bap),
                inc=(ds, 16), waits=[("ev", we)])
            if (wi * 16 + t) % 6 == 5:
                emit_b_chunk()
    DS_TERN = (kb.val("ds0"), kb.val("ds1"))
    AC_TERN = kb.val("ac")

    if stop_after == "prep":
        return _finish()

    while bst["next"] < 16:
        emit_b_chunk()
    MM_KV = kb.val("mm")
    kb.op("vector", lambda v: v.tensor_copy(out=kv_sb[:], in_=banks[4][0:128, 0:512]),
          inc=("ev", 1), waits=[("mm", MM_KV)])
    kb.op("vector", lambda v: v.tensor_copy(out=ks_sb[:], in_=banks[5][0:128, 0:4]),
          inc=("ev", 1))
    bank_ev[4] = [("ev", kb.val("ev"))]
    bank_ev[5] = [("ev", kb.val("ev"))]
    tap("kv", [128, 512], BF16); emit_tap("kv", kv_sb[:])
    tap("ks", [128, 4], BF16); emit_tap("ks", ks_sb[:])
    tap("stats", [128, 32]); emit_tap("stats", stats[:])

    if stop_after == "B":
        return _finish()

    arena_war_mm = MM_KV
    a2b_slot_mm = {0: 0, 1: 0}
    for q in range(NQ):
        sc0 = q * SCQ
        DGL_PREV = kb.val("dgl")
        DST_PREV = kb.val("dst")
        for i in range(SCQ):
            kb.op("sync", lambda s, i=i, sc0=sc0: s.dma_start(
                out=x_q[:, i * D:(i + 1) * D],
                in_=x_in[(sc0 + i) * 128:(sc0 + i + 1) * 128, :]),
                inc=("dgl", 16),
                waits=[("dgl", DGL_PREV), ("dst", DST_PREV)] + (
                    [("ac", AC_TERN)] if q == 0 else []))
        for dc in range(16):
            kb.op("sync", lambda s, dc=dc: s.dma_start(
                out=wq_tile(dc), in_=wq_bf[dc * 128:(dc + 1) * 128, :]),
                inc=("dgl", 16),
                waits=[("dgl", DGL_PREV), ("mm", arena_war_mm),
                       ("ds0", DS_WQO[0]), ("ds1", DS_WQO[1])] + (
                    [("ac", AC_TERN)] if q == 0 else []))
        DLS_XWQ = kb.val("dgl")

        h2tm_war_mm = kb.val("mm")
        for i in range(SCQ):
            sc = sc0 + i
            kb.op("scalar", lambda s_, i=i, sc=sc: s_.activation(
                out=h2tm[:], in_=x_q[:, i * D:(i + 1) * D], func=ACTF.Identity,
                scale=stats[:, 2 * sc:2 * sc + 1],
                bias=stats[:, 2 * sc + 1:2 * sc + 2]), inc=("ac", 1),
                waits=[("dgl", DLS_XWQ), ("mm", h2tm_war_mm)])
            wh = kb.val("ac")
            transpose_pass(16, lambda t: h2tm[:, t * 128:(t + 1) * 128],
                           d4_strided(h1f[:], i * 128, 16 * QT),
                           bank_base=0, first_waits=[("ac", wh)], nbanks=4)
            h2tm_war_mm = kb.val("mm")
        EV_H1F = kb.val("ev")

        elu_ac = {0: 0, 1: 0}
        elu_ev = {0: 0, 1: 0}
        for hc in range(16):
            p = hc % 2
            bnk = banks[2 + p]
            w0 = [("ev", EV_H1F), ("dgl", DLS_XWQ)] + bank_ev[2 + p]
            for dc in range(16):
                kb.op("tensor", lambda te, hc=hc, dc=dc, bnk=bnk: te.matmul(
                    bnk[0:128, 0:512], wq_tile(dc)[:, hc * 128:(hc + 1) * 128],
                    h1f[:, dc * 512:(dc + 1) * 512],
                    start=(dc == 0), stop=(dc == 15)),
                    inc=("mm", 1) if dc == 15 else None,
                    waits=w0 if dc == 0 else [])
            wq_mm = kb.val("mm")
            eu, ee, er = elu_tiles(p)
            kb.op("vector", lambda v, hc=hc, bnk=bnk, eu=eu: v.tensor_scalar(
                out=eu, in0=bnk[0:128, 0:512], scalar1=bq_sb[:, hc:hc + 1],
                scalar2=0.0, op0=ALU.add, op1=ALU.min), inc=("ev", 1),
                waits=[("mm", wq_mm), ("dbq", 16), ("ac", elu_ac[p])])
            wu = kb.val("ev")
            kb.op("scalar", lambda s_, hc=hc, bnk=bnk, er=er: s_.activation(
                out=er, in_=bnk[0:128, 0:512], func=ACTF.Relu,
                bias=bq_sb[:, hc:hc + 1], scale=1.0), inc=("ac", 1),
                waits=[("mm", wq_mm), ("ev", wu), ("ev", elu_ev[p])])
            wrel = kb.val("ac")
            kb.op("scalar", lambda s_, eu=eu, ee=ee: s_.activation(
                out=ee, in_=eu, func=ACTF.Exp), inc=("ac", 1))
            wee = kb.val("ac")
            elu_ac[p] = wee
            bank_ev[2 + p] = [("ev", wu), ("ac", wrel)]
            kb.op("vector", lambda v, hc=hc, ee=ee, er=er: v.tensor_tensor(
                out=sq_q[:, hc * 512:(hc + 1) * 512], in0=ee, in1=er,
                op=ALU.add), inc=("ev", 1), waits=[("ac", wee)])
            elu_ev[p] = kb.val("ev")
        EV_SQ = kb.val("ev")
        if q == TAPQ:
            tap("sq", [128, 16 * QT], BF16); emit_tap("sq", sq_q[:])

        w0 = [("ev", EV_SQ)] + bank_ev[4]
        for sl in range(4):
            for hc in range(16):
                kb.op("tensor", lambda te, sl=sl, hc=hc: te.matmul(
                    banks[4][0:128, sl * 16 + hc:sl * 16 + hc + 1],
                    sq_q[:, hc * 512 + sl * 128:hc * 512 + (sl + 1) * 128],
                    ks_sb[:, hc // 4:hc // 4 + 1], start=True, stop=True),
                    inc=("mm", 1) if (sl == 3 and hc == 15) else None,
                    waits=w0 if (sl == 0 and hc == 0) else [])
        MM_NRM = kb.val("mm")
        kb.op("vector", lambda v: v.tensor_scalar(
            out=r_sb[:], in0=banks[4][0:128, 0:64], scalar1=EPS_ATTN,
            scalar2=None, op0=ALU.add), inc=("ev", 1), waits=[("mm", MM_NRM)])
        wr = kb.val("ev")
        bank_ev[4] = [("ev", wr)]
        kb.op("vector", lambda v: v.reciprocal(out=r_sb[:], in_=r_sb[:]),
              inc=("ev", 1), waits=[("ev", wr)])
        EV_R = kb.val("ev")
        if q == TAPQ:
            tap("r", [128, 64]); emit_tap("r", r_sb[:])

        q2b_war_mm = kb.val("mm")
        for n_ in range(16):
            sl, hq = n_ // 4, n_ % 4
            p = n_ % 2
            bnk = banks[2 + p]
            w0 = [("ev", EV_SQ)] + bank_ev[2 + p]
            for hh in range(4):
                hc = hq * 4 + hh
                kb.op("tensor", lambda te, sl=sl, hc=hc, hh=hh, bnk=bnk: te.matmul(
                    bnk[0:128, hh * 128:(hh + 1) * 128],
                    sq_q[:, hc * 512 + sl * 128:hc * 512 + (sl + 1) * 128],
                    kv_sb[:, (hc // 4) * 128:(hc // 4 + 1) * 128],
                    start=(hh == 0), stop=(hh == 3)),
                    inc=("mm", 1) if hh == 3 else None,
                    waits=w0 if hh == 0 else [])
            wqkv = kb.val("mm")

            def build_R(v, sl=sl, hq=hq):
                ins = None
                for hh in range(4):
                    hc = hq * 4 + hh
                    ins = v.tensor_scalar(
                        out=tqf[:, 512 + hh * 128:512 + (hh + 1) * 128],
                        in0=ones_f32[:, 0:128],
                        scalar1=r_sb[:, sl * 16 + hc:sl * 16 + hc + 1],
                        scalar2=None, op0=ALU.mult)
                return ins
            kb.op("vector", build_R, inc=("ev", 1), waits=[("ev", EV_R)])
            wrt = kb.val("ev")
            kb.op("vector", lambda v, bnk=bnk: v.tensor_tensor(
                out=q2b[:, 0:512], in0=bnk[0:128, 0:512], in1=tqf[:, 512:1024],
                op=ALU.mult), inc=("ev", 1),
                waits=[("mm", wqkv), ("mm", q2b_war_mm), ("ev", wrt)])
            wtm = kb.val("ev")
            bank_ev[2 + p] = [("ev", wtm)]
            bti = 4 + (n_ % 4)
            bt = banks[bti]
            wtb0 = [("ev", wtm)] + bank_ev[bti]
            for hh in range(4):
                kb.op("tensor", lambda te, hh=hh, bt=bt: te.matmul(
                    bt[0:128, hh * 128:(hh + 1) * 128],
                    q2b[:, hh * 128:(hh + 1) * 128], ident[:],
                    start=(hh == 0), stop=(hh == 3)),
                    inc=("mm", 1) if hh == 3 else None,
                    waits=wtb0 if hh == 0 else [])
            wtb = kb.val("mm")
            q2b_war_mm = wtb
            oq = bass.AP(qkp[:].tensor, (hq * 4) * 512 + sl * 128,
                         [[16 * QT, 128], [512, 4], [1, 128]])
            iq = bass.AP(bt[:].tensor, 0, [[512, 128], [128, 4], [1, 128]])
            kb.op("vector", lambda v, oq=oq, iq=iq: v.tensor_copy(
                out=oq, in_=iq), inc=("ev", 1), waits=[("mm", wtb)])
            bank_ev[bti] = [("ev", kb.val("ev"))]
        EV_QKP = kb.val("ev")
        if q == TAPQ:
            tap("qkp", [128, 16 * QT], BF16); emit_tap("qkp", qkp[:])

        if stop_after == "C":
            return _finish()
        for fg in range(4):
            p = fg % 2
            dp = "dp0" if p == 0 else "dp1"
            so = p * 8192
            for gc in range(16):
                kb.op("sync", lambda s, fg=fg, gc=gc, so=so: s.dma_start(
                    out=a2b[:, so + gc * 512:so + (gc + 1) * 512],
                    in_=wo_bf[gc * 128:(gc + 1) * 128, fg * 512:(fg + 1) * 512]),
                    inc=(dp, 16),
                    waits=[("ds0", DS_TERN[0]), ("ds1", DS_TERN[1]),
                           ("mm", a2b_slot_mm[p])])
            DLS_WO = kb.val(dp)
            for i in range(SCQ):
                bp = i % 2
                bnk = banks[2 + bp]
                w0 = [(dp, DLS_WO), ("ev", EV_QKP)] + bank_ev[2 + bp]
                for gc in range(16):
                    kb.op("tensor", lambda te, i=i, gc=gc, bnk=bnk, so=so: te.matmul(
                        bnk[0:128, 0:512],
                        qkp[:, gc * 512 + i * 128:gc * 512 + (i + 1) * 128],
                        a2b[:, so + gc * 512:so + (gc + 1) * 512],
                        start=(gc == 0), stop=(gc == 15)),
                        inc=("mm", 1) if gc == 15 else None,
                        waits=w0 if gc == 0 else [])
                wo_mm = kb.val("mm")
                kb.op("vector", lambda v, i=i, fg=fg, bnk=bnk: v.tensor_tensor(
                    out=x_q[:, i * D + fg * 512:i * D + (fg + 1) * 512],
                    in0=bnk[0:128, 0:512],
                    in1=x_q[:, i * D + fg * 512:i * D + (fg + 1) * 512],
                    op=ALU.add), inc=("ev", 1), waits=[("mm", wo_mm)])
                bank_ev[2 + bp] = [("ev", kb.val("ev"))]
            a2b_slot_mm[p] = kb.val("mm")
        EV_X1 = kb.val("ev")
        if q == TAPQ:
            tap("x1", [128, SCQ * D]); emit_tap("x1", x_q[:])

        if stop_after == "E":
            return _finish()
        wq_mm_last = kb.val("mm")
        h2tm_war_mm = kb.val("mm")
        cast_ac_last = 0
        q1b_war_mm = kb.val("mm")
        lad_slot = {0: kb.val("ev"), 1: kb.val("ev")}
        q1_pend = None
        for i in range(SCQ):
            xi = x_q[:, i * D:(i + 1) * D]
            ln_stats(xi, ln2st[:, 2 * i:2 * i + 1], ln2st[:, 2 * i + 1:2 * i + 2],
                     first_waits=[("ev", EV_X1)] if i == 0 else [])
            wst = kb.val("ev")
            kb.op("scalar", lambda s_, xi=xi, i=i: s_.activation(
                out=h2tm[:], in_=xi, func=ACTF.Identity,
                scale=ln2st[:, 2 * i:2 * i + 1],
                bias=ln2st[:, 2 * i + 1:2 * i + 2]),
                inc=("ac", 1), waits=[("ev", wst), ("mm", h2tm_war_mm)])
            wh = kb.val("ac")
            d0 = transpose_pass(16, lambda t: h2tm[:, t * 128:(t + 1) * 128],
                                d4_contig(scr_bf, base=2048), bank_base=0,
                                first_waits=[("ac", wh)], evac="alt", nbanks=4)
            d1 = transpose_pass(16, lambda t: h2f[:, t * 128:(t + 1) * 128],
                                d4_inter(pa1[:], 16, D), bank_base=0, rhs=h128[:],
                                first_waits=d0, evac="alt", nbanks=4)
            d2 = transpose_pass(16, lambda t: pa1[:, t * 128:(t + 1) * 128],
                                d4_contig(pa1), bank_base=0,
                                first_waits=d1, evac="alt", nbanks=4)
            h2tm_war_mm = kb.val("mm")
            for b in range(4):
                w0 = list(d2) + bank_ev[2 + b] + (
                    [("hb", HB_DONE)] if (q == 0 and i == 0 and b == 0) else [])
                for c4 in range(4):
                    c2 = b * 4 + c4
                    kb.op("tensor", lambda te, c2=c2, c4=c4, b=b: te.matmul(
                        banks[2 + b][0:128, c4 * 128:(c4 + 1) * 128],
                        pa1[:, c2 * 128:(c2 + 1) * 128], hB1[:],
                        start=True, stop=True),
                        inc=("mm", 1) if c4 == 3 else None,
                        waits=w0 if c4 == 0 else [])
            wpb = kb.val("mm")

            def absred1(v):
                ins = None
                for b in range(4):
                    ins = v.reduce_max(out=st_[:, b:b + 1],
                                       in_=banks[2 + b][0:128, 0:512],
                                       axis=AX.X, apply_absolute_value=True)
                return ins
            kb.op("vector", absred1, inc=("ev", 1), waits=[("mm", wpb)])
            wam0 = kb.val("ev")
            kb.op("vector", lambda v: v.reduce_max(
                out=st_[:, 4:5], in_=st_[:, 0:4], axis=AX.X), inc=("ev", 1),
                waits=[("ev", wam0)])
            wam = kb.val("ev")
            kb.op("vector", lambda v, i=i: v.tensor_scalar(
                out=st_[:, 7:8], in0=st_[:, 4:5], scalar1=T1CLIP,
                scalar2=None, op0=ALU.max), inc=("ev", 1), waits=[("ev", wam)])
            wa1 = kb.val("ev")
            kb.op("vector", lambda v: v.reciprocal(
                out=st_[:, 5:6], in_=st_[:, 7:8]), inc=("ev", 1),
                waits=[("ev", wa1)])
            wrc = kb.val("ev")
            kb.op("vector", lambda v: v.tensor_scalar(
                out=st_[:, 6:7], in0=st_[:, 5:6], scalar1=127.0, scalar2=None,
                op0=ALU.mult), inc=("ev", 1),
                waits=[("ev", wrc), ("ac", cast_ac_last)])
            wscq = kb.val("ev")
            kb.op("vector", lambda v, i=i: v.reciprocal(
                out=a1c_sb[:, i:i + 1], in_=st_[:, 6:7]), inc=("ev", 1),
                waits=[("ev", wscq)])
            if q1_pend is not None:
                ip, wlad_p = q1_pend
                q1f_done = transpose_pass(
                    16, lambda t: q1b[:, t * 128:(t + 1) * 128],
                    d4_strided(arena[:], 16384 + ip * 128, 32768),
                    bank_base=6, evac="alt", nbanks=4,
                    first_waits=[("ev", wlad_p), ("mm", wq_mm_last)])
                q1b_war_mm = kb.val("mm")
            for b in range(4):
                slot = b % 2
                kb.op("scalar", lambda s_, b=b, slot=slot: s_.activation(
                    out=qi32[:, slot * 512:(slot + 1) * 512],
                    in_=banks[2 + b][0:128, 0:512], func=ACTF.Identity,
                    scale=st_[:, 6:7], bias=cst[:, 2:3]), inc=("ac", 1),
                    waits=[("ev", max(wscq, lad_slot[slot]))])
                kb.op("scalar", lambda s_, b=b, slot=slot: s_.activation(
                    out=tqf[:, slot * 512:(slot + 1) * 512],
                    in_=banks[2 + b][0:128, 0:512], func=ACTF.Identity,
                    scale=st_[:, 6:7], bias=cst[:, 3:4]), inc=("ac", 1))
                wq0 = kb.val("ac")
                bank_ev[2 + b] = [("ev", wam0), ("ac", wq0)]
                quant_tail(512, slot, q1b[:, b * 512:(b + 1) * 512], wq0,
                           extra_final=[("mm", q1b_war_mm)])
                lad_slot[slot] = kb.val("ev")
            cast_ac_last = kb.val("ac")
            q1_pend = (i, kb.val("ev"))
        ip, wlad_p = q1_pend
        q1f_done = transpose_pass(
            16, lambda t: q1b[:, t * 128:(t + 1) * 128],
            d4_strided(arena[:], 16384 + ip * 128, 32768),
            bank_base=6, evac="alt", nbanks=4,
            first_waits=[("ev", wlad_p), ("mm", wq_mm_last)])
        if q == TAPQ:
            tap("q1f", [128, 8192], BF16)
            emit_tap("q1f", arena[:, 16384:16384 + 8192])
            tap("a1c", [128, 4]); emit_tap("a1c", a1c_sb[:])

        if stop_after == "F1":
            return _finish()
        gq_war_mm = wq_mm_last
        rl_ev = {0: kb.val("ev"), 1: kb.val("ev")}
        for blk in range(8):
            p = blk % 2
            dp = "dp0" if p == 0 else "dp1"
            so = p * 8192
            for dc in range(16):
                kb.op("sync", lambda s, blk=blk, dc=dc, so=so: s.dma_start(
                    out=a2b[:, so + dc * 512:so + (dc + 1) * 512],
                    in_=wup_t[dc * 128:(dc + 1) * 128, blk * 512:(blk + 1) * 512]),
                    inc=(dp, 16),
                    waits=[("ds0", DS_TERN[0]), ("ds1", DS_TERN[1]),
                           ("mm", a2b_slot_mm[p])])
            DLS_UP = kb.val(dp)
            for fc4 in range(4):
                fc = blk * 4 + fc4
                bp = fc4 % 2
                bnk = banks[2 + bp]
                w0 = [(dp, DLS_UP)] + list(q1f_done) + bank_ev[2 + bp]
                for dc in range(16):
                    kb.op("tensor", lambda te, fc4=fc4, dc=dc, bnk=bnk, so=so: te.matmul(
                        bnk[0:128, 0:512],
                        a2b[:, so + dc * 512 + fc4 * 128:so + dc * 512 + (fc4 + 1) * 128],
                        q1f_sl(dc, 0, 512), start=(dc == 0), stop=(dc == 15)),
                        inc=("mm", 1) if dc == 15 else None,
                        waits=w0 if dc == 0 else [])
                wup_mm = kb.val("mm")
                rsc = y2f[:, bp * 512:(bp + 1) * 512]
                kb.op("vector", lambda v, bnk=bnk, rsc=rsc: v.tensor_single_scalar(
                    out=rsc, in_=bnk[0:128, 0:512], scalar=0.0,
                    op=ALU.max), inc=("ev", 1),
                    waits=[("mm", wup_mm), ("ev", rl_ev[bp])])
                wrl = kb.val("ev")
                kb.op("vector", lambda v, fc=fc, bnk=bnk, rsc=rsc: v.tensor_tensor(
                    out=gq_sl(fc, 0, 512), in0=rsc,
                    in1=bnk[0:128, 0:512], op=ALU.mult),
                    inc=("ev", 1),
                    waits=[("ev", wrl), ("mm", gq_war_mm)])
                rl_ev[bp] = kb.val("ev")
                bank_ev[2 + bp] = [("ev", kb.val("ev"))]
            a2b_slot_mm[p] = kb.val("mm")
        EV_GQ = kb.val("ev")
        if q == TAPQ:
            tap("gq", [128, 16384], BF16); emit_tap("gq", arena[:, 0:16384])

        if stop_after == "F2":
            return _finish()
        cast_ac_last = kb.val("ac")
        q2b_war_mm = kb.val("mm")
        lad_slot = {0: kb.val("ev"), 1: kb.val("ev")}
        pa2_war_mm = kb.val("mm")
        q2_pend = None
        for i in range(SCQ):
            dA = transpose_pass(
                32, lambda t, i=i: gq_sl(t, i * 128, (i + 1) * 128),
                d4_inter(pa2, 32, FF), bank_base=0, rhs=h128[:],
                first_waits=[("ev", EV_GQ), ("mm", pa2_war_mm)], evac="alt",
                nbanks=4)
            dT = transpose_pass(32, lambda t: pa2[:, t * 128:(t + 1) * 128],
                                d4_contig(scr_bf), bank_base=0,
                                first_waits=dA, evac="alt", nbanks=4)
            for piece in range(2):
                for b in range(4):
                    w0 = list(dT) + bank_ev[2 + b]
                    for c4 in range(4):
                        cc = b * 4 + c4
                        c2 = piece * 16 + cc
                        kb.op("tensor", lambda te, c2=c2, c4=c4, b=b: te.matmul(
                            banks[2 + b][0:128, c4 * 128:(c4 + 1) * 128],
                            pa2[:, c2 * 128:(c2 + 1) * 128], hB2[:],
                            start=True, stop=True),
                            inc=("mm", 1) if c4 == 3 else None,
                            waits=w0 if c4 == 0 else [])
                wpb = kb.val("mm")
                for b in range(4):
                    kb.op("vector", lambda v, piece=piece, b=b: v.reduce_max(
                        out=st_[:, piece * 4 + b:piece * 4 + b + 1],
                        in_=banks[2 + b][0:128, 0:512], axis=AX.X,
                        apply_absolute_value=True), inc=("ev", 1),
                        waits=[("mm", wpb)] if b == 0 else [])
                    wred = kb.val("ev")
                    kb.op("scalar", lambda s_, piece=piece, b=b: s_.activation(
                        out=y2f[:, piece * 2048 + b * 512:piece * 2048 + (b + 1) * 512],
                        in_=banks[2 + b][0:128, 0:512], func=ACTF.Identity),
                        inc=("ac", 1), waits=[("mm", wpb), ("ev", wred)])
                    bank_ev[2 + b] = [("ev", wred), ("ac", kb.val("ac"))]
            wy2 = kb.val("ev")
            wy2a = kb.val("ac")
            kb.op("vector", lambda v: v.reduce_max(
                out=st_[:, 8:9], in_=st_[:, 0:8], axis=AX.X), inc=("ev", 1),
                waits=[("ev", wy2)])
            w1_ = kb.val("ev")
            kb.op("vector", lambda v: v.tensor_scalar(
                out=st_[:, 9:10], in0=st_[:, 8:9], scalar1=T2CLIP, scalar2=None,
                op0=ALU.max), inc=("ev", 1), waits=[("ev", w1_)])
            wa2 = kb.val("ev")
            kb.op("vector", lambda v: v.reciprocal(
                out=st_[:, 10:11], in_=st_[:, 9:10]), inc=("ev", 1),
                waits=[("ev", wa2)])
            wrc2 = kb.val("ev")
            kb.op("vector", lambda v: v.tensor_scalar(
                out=st_[:, 11:12], in0=st_[:, 10:11], scalar1=127.0, scalar2=None,
                op0=ALU.mult), inc=("ev", 1),
                waits=[("ev", wrc2), ("ac", cast_ac_last)])
            wsc2a = kb.val("ev")
            kb.op("vector", lambda v: v.reciprocal(
                out=st_[:, 9:10], in_=st_[:, 11:12]), inc=("ev", 1),
                waits=[("ev", wsc2a)])
            wsc2 = kb.val("ev")
            kb.op("vector", lambda v, i=i: v.tensor_tensor(
                out=st_[:, 12:13], in0=a1c_sb[:, i:i + 1], in1=a1c_sb[:, i:i + 1],
                op=ALU.mult), inc=("ev", 1), waits=[("ev", wsc2)])
            wf1 = kb.val("ev")
            kb.op("vector", lambda v: v.tensor_tensor(
                out=st_[:, 13:14], in0=st_[:, 12:13], in1=st_[:, 9:10],
                op=ALU.mult), inc=("ev", 1), waits=[("ev", wf1)])
            wf2 = kb.val("ev")
            kb.op("vector", lambda v: v.tensor_tensor(
                out=st_[:, 14:15], in0=st_[:, 13:14], in1=wsc[:, 0:1],
                op=ALU.mult), inc=("ev", 1), waits=[("ev", wf2)])
            wf3 = kb.val("ev")
            kb.op("vector", lambda v: v.tensor_tensor(
                out=st_[:, 15:16], in0=st_[:, 14:15], in1=wsc[:, 0:1],
                op=ALU.mult), inc=("ev", 1), waits=[("ev", wf3)])
            wf4 = kb.val("ev")
            kb.op("vector", lambda v: v.tensor_tensor(
                out=st_[:, 12:13], in0=st_[:, 15:16], in1=wsc[:, 2:3],
                op=ALU.mult), inc=("ev", 1), waits=[("ev", wf4)])
            wf5 = kb.val("ev")
            kb.op("vector", lambda v, i=i: v.tensor_scalar(
                out=fs_sb[:, i:i + 1], in0=st_[:, 12:13],
                scalar1=7.62939453125e-06, scalar2=None,
                op0=ALU.mult), inc=("ev", 1), waits=[("ev", wf5)])
            if q2_pend is not None:
                ip, wlad_p = q2_pend
                q2f_done = transpose_pass(
                    32, lambda t: q2b[:, t * 128:(t + 1) * 128],
                    d4_strided(arena[:], ip * 128, 32768),
                    bank_base=6, evac="alt", nbanks=4,
                    first_waits=[("ev", wlad_p)])
                q2b_war_mm = kb.val("mm")
                pa2_war_mm = kb.val("mm")
            for b in range(8):
                slot = b % 2
                kb.op("scalar", lambda s_, b=b, slot=slot: s_.activation(
                    out=qi32[:, slot * 512:(slot + 1) * 512],
                    in_=y2f[:, b * 512:(b + 1) * 512], func=ACTF.Identity,
                    scale=st_[:, 11:12], bias=cst[:, 2:3]), inc=("ac", 1),
                    waits=[("ev", max(wsc2, lad_slot[slot])), ("ac", wy2a)])
                kb.op("scalar", lambda s_, b=b, slot=slot: s_.activation(
                    out=tqf[:, slot * 512:(slot + 1) * 512],
                    in_=y2f[:, b * 512:(b + 1) * 512], func=ACTF.Identity,
                    scale=st_[:, 11:12], bias=cst[:, 3:4]), inc=("ac", 1))
                wq0 = kb.val("ac")
                quant_tail(512, slot, q2b[:, b * 512:(b + 1) * 512], wq0,
                           extra_final=[("mm", q2b_war_mm)])
                lad_slot[slot] = kb.val("ev")
            cast_ac_last = kb.val("ac")
            q2_pend = (i, kb.val("ev"))
        ip, wlad_p = q2_pend
        q2f_done = transpose_pass(
            32, lambda t: q2b[:, t * 128:(t + 1) * 128],
            d4_strided(arena[:], ip * 128, 32768),
            bank_base=6, evac="alt", nbanks=4,
            first_waits=[("ev", wlad_p)])
        if stop_after == "F3":
            return _finish()
        for fg in range(4):
            for hf in range(2):
                dp = "dp0" if hf == 0 else "dp1"
                so = hf * 8192
                for d2 in range(16):
                    d2c = hf * 16 + d2
                    kb.op("sync", lambda s, fg=fg, d2=d2, d2c=d2c, so=so: s.dma_start(
                        out=a2b[:, so + d2 * 512:so + (d2 + 1) * 512],
                        in_=wdn_t[d2c * 128:(d2c + 1) * 128, fg * 512:(fg + 1) * 512]),
                        inc=(dp, 16),
                        waits=[("ds0", DS_TERN[0]), ("ds1", DS_TERN[1]),
                               ("mm", a2b_slot_mm[hf])])
            DLS_DN = (kb.val("dp0"), kb.val("dp1"))
            for hf in range(2):
                dp = "dp0" if hf == 0 else "dp1"
                so = hf * 8192
                for i in range(SCQ):
                    bnk = banks[2 + i]
                    w0 = [(dp, DLS_DN[hf])] + list(q2f_done) + (
                        bank_ev[2 + i] if hf == 0 else [])
                    for d2 in range(16):
                        d2c = hf * 16 + d2
                        kb.op("tensor", lambda te, i=i, d2=d2, d2c=d2c, bnk=bnk, so=so, hf=hf: te.matmul(
                            bnk[0:128, 0:512],
                            arena[:, d2c * 512 + i * 128:d2c * 512 + (i + 1) * 128],
                            a2b[:, so + d2 * 512:so + (d2 + 1) * 512],
                            start=(hf == 0 and d2 == 0), stop=(hf == 1 and d2 == 15)),
                            inc=("mm", 1) if d2 == 15 else None,
                            waits=w0 if d2 == 0 else [])
                a2b_slot_mm[hf] = kb.val("mm")
            wdn_mm = kb.val("mm")
            for i in range(SCQ):
                kb.op("vector", lambda v, i=i, fg=fg: v.scalar_tensor_tensor(
                    out=x_q[:, i * D + fg * 512:i * D + (fg + 1) * 512],
                    in0=banks[2 + i][0:128, 0:512], scalar=fs_sb[:, i:i + 1],
                    in1=x_q[:, i * D + fg * 512:i * D + (fg + 1) * 512],
                    op0=ALU.mult, op1=ALU.add), inc=("ev", 1),
                    waits=[("mm", wdn_mm)])
                bank_ev[2 + i] = [("ev", kb.val("ev"))]
        EV_Y = kb.val("ev")
        arena_war_mm = kb.val("mm")
        for i in range(SCQ):
            kb.op("sync", lambda s, i=i, sc0=sc0: s.dma_start(
                out=y_out[(sc0 + i) * 128:(sc0 + i + 1) * 128, :],
                in_=x_q[:, i * D:(i + 1) * D]), inc=("dst", 16),
                waits=[("ev", EV_Y)] if i == 0 else [])

    DST_ALL = kb.val("dst")
    kb.op("sync", lambda s: s.wait_ge(kb.sems["dst"], DST_ALL))

    return _finish()


def host_inputs(L0_attn_Wq, L0_attn_bq, L0_attn_Wk, L0_attn_bk, L0_attn_Wv,
                L0_attn_bv, L0_attn_Wo, L0_ffn_w_up, L0_ffn_w_down):
    asc = np.ascontiguousarray
    return {
        "wqT": asc(np.asarray(L0_attn_Wq, np.float32).T),
        "wkT": asc(np.asarray(L0_attn_Wk, np.float32).T),
        "wvT": asc(np.asarray(L0_attn_Wv, np.float32).T),
        "woT": asc(np.asarray(L0_attn_Wo, np.float32).T),
        "wupT": asc(np.asarray(L0_ffn_w_up, np.float32).T[PERM_F1, :]),
        "wdnT": asc(np.asarray(L0_ffn_w_down, np.float32).T[PERM_F2, :]),
        "bq_in": asc(np.asarray(L0_attn_bq, np.float32).reshape(16, 128).T),
        "bk_in": asc(np.asarray(L0_attn_bk, np.float32).reshape(1, 512)),
        "bv_in": asc(np.asarray(L0_attn_bv, np.float32).reshape(1, 512)),
    }


def kernel(L0_attn_hidden_states, L0_attn_Wq, L0_attn_bq, L0_attn_Wk, L0_attn_bk,
           L0_attn_Wv, L0_attn_bv, L0_attn_Wo, L0_ffn_w_up, L0_ffn_w_down):
    asc = np.ascontiguousarray
    x = np.asarray(L0_attn_hidden_states, np.float32)
    common = host_inputs(L0_attn_Wq, L0_attn_bq, L0_attn_Wk, L0_attn_bk,
                         L0_attn_Wv, L0_attn_bv, L0_attn_Wo, L0_ffn_w_up,
                         L0_ffn_w_down)
    nc, _ = build()
    in_maps = [dict(common, x_in=asc(x[c])) for c in range(8)]
    res = run_bass_kernel_spmd(nc, in_maps, core_ids=list(range(8)))
    return np.stack([res.results[c]["y_out"] for c in range(8)], axis=0)
